# version 19 (final)
# baseline (speedup 1.0000x reference)
import numpy as np
from contextlib import ExitStack
from ml_dtypes import bfloat16
import concourse.bass as bass
import concourse.mybir as mybir
from concourse.bass_utils import run_bass_kernel_spmd

F32 = mybir.dt.float32
BF16 = mybir.dt.bfloat16
AF = mybir.ActivationFunctionType
ALU = mybir.AluOpType

T = 1024
FIN = 32
HID = 4096
NG = 8
HG = 4
D = 128
NC8 = 8
CH = 128
EPS = 1e-5
RMS_EPS = 1e-5

ENGS = ("tensor", "vector", "scalar", "gpsimd", "sync")

_DATAPATH_OPS = {
    "tensor_tensor", "tensor_add", "tensor_sub", "tensor_mul", "tensor_max",
    "tensor_scalar", "tensor_single_scalar", "tensor_scalar_add",
    "tensor_scalar_sub", "tensor_scalar_mul", "tensor_scalar_max",
    "tensor_scalar_min", "tensor_relu", "tensor_reduce", "reduce_max",
    "reduce_sum", "scalar_tensor_tensor", "tensor_tensor_scan", "tensor_copy",
    "memset", "memzero", "reciprocal", "select", "transpose", "bn_stats",
    "bn_aggr", "activation", "mul", "add", "copy", "square", "sqrt", "sign",
    "tensor_tensor_reduce", "copy_predicated", "stream_shuffle",
}


class DrainProxy:
    def __init__(self, eng):
        object.__setattr__(self, "_e", eng)

    def __getattr__(self, name):
        a = getattr(self._e, name)
        if name in _DATAPATH_OPS and callable(a):
            def wrapped(*args, _a=a, **kw):
                self._e.drain()
                return _a(*args, **kw)
            return wrapped
        return a


class Sem:
    def __init__(self, h):
        self.h = h
        self.n = 0

    def inc(self, a=1):
        self.n += a
        return self.n


class Blk:
    BAR = None
    STREAM = None

    def __init__(self, nc):
        self.nc = nc
        self.q = {e: [] for e in ENGS}

    def pe(self, fn): self.q["tensor"].append(fn)
    def dv(self, fn): self.q["vector"].append(fn)
    def ac(self, fn): self.q["scalar"].append(fn)
    def gp(self, fn): self.q["gpsimd"].append(fn)
    def sy(self, fn): self.q["sync"].append(fn)

    def emit(self, bar=True):
        if bar:
            bsem, cnt = Blk.BAR
            cnt[0] += 1
            tgt = 5 * cnt[0]
            def barf(eng, tgt=tgt):
                eng.drain().then_inc(bsem, 1)
                eng.wait_ge(bsem, tgt)
            tail = [barf]
        else:
            tail = []
        for ename in ENGS:
            Blk.STREAM[ename] += self.q[ename] + tail

    @classmethod
    def flush(cls, nc):
        with nc.Block() as b:
            for ename in ENGS:
                fns = cls.STREAM[ename]
                if not fns:
                    continue
                def body(eng, fns=fns, ename=ename):
                    if ename in ("vector", "scalar"):
                        eng = DrainProxy(eng)
                    for f in fns:
                        f(eng)
                getattr(b, ename)(body)
        cls.STREAM = {e: [] for e in ENGS}


def build(debug_taps=()):
    nc = bass.Bass("TRN2", target_bir_lowering=False, debug=False)
    taps = set(debug_taps)
    dbg_names = []

    dx = nc.dram_tensor("dx", [HID, T], BF16, kind="ExternalInput").ap()
    dwq = nc.dram_tensor("dwq", [32 * 128, 4096], BF16, kind="ExternalInput").ap()
    dwg = nc.dram_tensor("dwg", [32 * 128, 4096], BF16, kind="ExternalInput").ap()
    dwo = nc.dram_tensor("dwo", [32 * 128, 4096], BF16, kind="ExternalInput").ap()
    dwvk = nc.dram_tensor("dwvk", [4 * 32 * 128, 512], BF16, kind="ExternalInput").ap()
    dwin = nc.dram_tensor("dwin", [128, 1024], BF16, kind="ExternalInput").ap()
    daall = nc.dram_tensor("daall", [128, 4096], BF16, kind="ExternalInput").ap()
    droute = nc.dram_tensor("droute", [128, 256], BF16, kind="ExternalInput").ap()
    dbw = nc.dram_tensor("dbw", [128, HID], BF16, kind="ExternalInput").ap()
    dcosf = nc.dram_tensor("dcosf", [128, T], BF16, kind="ExternalInput").ap()
    dsinf = nc.dram_tensor("dsinf", [128, T], BF16, kind="ExternalInput").ap()
    dcost = nc.dram_tensor("dcost", [128, NC8 * D], BF16, kind="ExternalInput").ap()
    dsint = nc.dram_tensor("dsint", [128, NC8 * D], BF16, kind="ExternalInput").ap()
    dpsw = nc.dram_tensor("dpsw", [128, 128], BF16, kind="ExternalInput").ap()
    didf = nc.dram_tensor("didf", [128, 128], F32, kind="ExternalInput").ap()
    didb = nc.dram_tensor("didb", [128, 128], BF16, kind="ExternalInput").ap()
    ds127 = nc.dram_tensor("ds127", [128, 128], F32, kind="ExternalInput").ap()
    dsel4 = nc.dram_tensor("dsel4", [4, 512], F32, kind="ExternalInput").ap()
    deye = nc.dram_tensor("deye", [128, 1024], BF16, kind="ExternalInput").ap()
    dmskn = nc.dram_tensor("dmskn", [128, 128], BF16, kind="ExternalInput").ap()
    ddtb = nc.dram_tensor("ddtb", [32, 1], F32, kind="ExternalInput").ap()
    dalog = nc.dram_tensor("dalog", [32, 1], F32, kind="ExternalInput").ap()
    dgnw = nc.dram_tensor("dgnw", [128, 1], F32, kind="ExternalInput").ap()
    dsendm = nc.dram_tensor("dsendm", [128, 1], F32, kind="ExternalInput").ap()
    drecvm = nc.dram_tensor("drecvm", [128, 1], F32, kind="ExternalInput").ap()
    dout = nc.dram_tensor("dout", [HID, T], F32, kind="ExternalOutput").ap()

    xpark = nc.dram_tensor("xpark", [HID, T], BF16).ap()
    ypark = nc.dram_tensor("ypark", [NG, 128, 4096], BF16).ap()
    ccin = nc.dram_tensor("ccin", [128, 512], F32)
    ccout = nc.dram_tensor("ccout", [128, 512], F32)

    with ExitStack() as ST:
        E = ST.enter_context
        R1 = E(nc.sbuf_tensor("R1", [128, 32768], BF16))
        R2 = E(nc.sbuf_tensor("R2", [128, 32768], BF16))
        R3 = E(nc.sbuf_tensor("R3", [128, 8192], BF16))
        KTB = E(nc.sbuf_tensor("KTB", [128, 8192], BF16))
        WSA = E(nc.sbuf_tensor("WSA", [128, 4096], BF16))
        WSB = E(nc.sbuf_tensor("WSB", [128, 4096], BF16))
        P4T = E(nc.sbuf_tensor("P4T", [128, 3072], BF16))
        SCR = E(nc.sbuf_tensor("SCR", [128, 5120], F32))
        SEL4 = E(nc.sbuf_tensor("SEL4", [4, 512], F32))
        IDF = E(nc.sbuf_tensor("IDF", [128, 128], F32))
        IDB = E(nc.sbuf_tensor("IDB", [128, 128], BF16))
        S127 = E(nc.sbuf_tensor("S127", [128, 128], F32))
        PSW = E(nc.sbuf_tensor("PSW", [128, 128], BF16))
        O1F = E(nc.sbuf_tensor("O1F", [1, 128], F32))
        OCB = E(nc.sbuf_tensor("OCB", [128, 1], BF16))
        ZR32 = E(nc.sbuf_tensor("ZR32", [32, 128], F32))
        DTBC = E(nc.sbuf_tensor("DTBC", [32, 1], F32))
        ALGC = E(nc.sbuf_tensor("ALGC", [32, 1], F32))
        ACOL = E(nc.sbuf_tensor("ACOL", [32, 1], F32))
        GNWC = E(nc.sbuf_tensor("GNWC", [128, 1], F32))
        SENDM = E(nc.sbuf_tensor("SENDM", [128, 1], F32))
        RECVM = E(nc.sbuf_tensor("RECVM", [128, 1], F32))

        QG = R1[:, 0:4096]
        QS = R1[:, 4096:8192]
        YGRB = R1[:, 8192:12288]
        YGR = R1[:, 12288:16384]
        CST = R1[:, 16384:20480]
        A4SB = R1[:, 20480:21504].bitcast(F32)
        DEC4B = R1[:, 21504:22528].bitcast(F32)
        PRV = R1[:, 24576:25600].bitcast(F32)
        CARRY = R1[:, 25600:26624].bitcast(F32)
        CRB = R1[:, 26624:27648].bitcast(F32)
        COSB = R1[:, 27648:28672]
        SINM = R1[:, 28672:29696]
        EYE = R1[:, 29696:30720]
        MSKN = R1[:, 30720:30848]
        CTT = R1[:, 30848:31872]
        STT = R1[:, 16384:17408]
        PRVB2 = R1[:, 30848:31360]
        A4SC = R1[:, 22528:23552].bitcast(F32)
        PRVB2B = R1[:, 20480:20992]
        A4SD = R1[:, 23552:24576].bitcast(F32)
        PRV2 = R1[:, 21504:22528].bitcast(F32)
        SQY2 = R1[:, 31360:32384]

        DTF = SCR[0:32, 0:1024]
        GFT = SCR[0:32, 1024:2048]
        T1 = SCR[0:32, 2048:3072]
        GR4 = SCR[0:4, 2048:3072]
        NGT = SCR[:, 3072:3328]
        DTTc = SCR[:, 3328:3584]
        EDT = SCR[:, 3584:3840]
        CDB = SCR[:, 3840:4096]
        A4S = SCR[:, 4096:4608]
        E1B = A4S
        DEC4 = SCR[:, 4608:5120]
        ROWA = SCR[0:1, 0:1024]
        ROWB = SCR[0:1, 1024:2048]
        LG8 = SCR[0:8, 2048:3072]
        MXC = SCR[0:128, 5119:5120]
        SSQ = SCR[0:32, 0:1024]
        RSTD = SCR[0:32, 1024:2048]
        CMS = SCR[:, 4096:4608]

        QRW = P4T[:, 0:1024]
        SM4A = P4T[:, 0:512]
        SM4B = P4T[:, 512:1024]
        PRVB = P4T[:, 0:512]
        SQY = P4T[:, 0:1024]
        O2S = P4T[:, 1024:2048]
        KD4A = P4T[:, 1024:1536]
        KD4B = P4T[:, 1536:2048]
        KFG = P4T[:, 2048:3072]

        VTB = R3
        SQTa = R3[:, 0:1024]
        SQTb = R3[:, 1024:2048]
        WHF = R3[:, 2048:3072]
        WHT = R3[:, 3072:3200]
        C0S = R3[:, 3328:4352]
        C1S = R3[:, 4352:5376]
        OTMPa = R3[:, 5376:7424].bitcast(F32)
        OTMPb = R3[:, 0:2048].bitcast(F32)
        KR1 = P4T[:, 0:1024]
        KR2 = P4T[:, 1024:2048]
        YHp = (R3[:, 0:1024], R3[:, 1024:2048])
        SILt = R3[:, 2048:3072]
        OYt = R3[:, 3072:4096]
        XPTp = (R3[:, 0:1024], R3[:, 1024:2048])

        sW = Sem(E(nc.semaphore("sW")))
        sX = Sem(E(nc.semaphore("sX")))
        sP = Sem(E(nc.semaphore("sP")))
        sV = Sem(E(nc.semaphore("sV")))
        sA = Sem(E(nc.semaphore("sA")))
        sC = Sem(E(nc.semaphore("sC")))
        sB = E(nc.semaphore("sB"))
        Blk.BAR = (sB, [0])
        Blk.STREAM = {e: [] for e in ENGS}
        sSL = [Sem(E(nc.semaphore(f"sS{i}"))) for i in range(12)]
        sQW = [Sem(E(nc.semaphore(f"sQW{i}"))) for i in range(2)]
        sG = Sem(E(nc.semaphore("sG")))
        sPa = Sem(E(nc.semaphore("sPa")))

        def dbg_tap(name, ap):
            if name not in taps:
                return
            o = nc.dram_tensor("dbg_" + name, list(ap.shape), ap.dtype,
                               kind="ExternalOutput").ap()
            dbg_names.append(name)
            k = Blk(nc)
            tgt = sX.inc(16)
            k.sy(lambda s, ap=ap, o=o, tgt=tgt: (
                s.dma_start(out=o, in_=ap).then_inc(sX.h, 16),
                s.wait_ge(sX.h, tgt)))
            k.emit()

        k = Blk(nc)
        def b0_gp(g):
            for i in range(4):
                src = dx[i * 1024:(i + 1) * 1024, :].rearrange("(f p) t -> p f t", p=128)
                dst = R1[:, i * 8192:(i + 1) * 8192].rearrange("p (f t) -> p f t", f=8)
                g.dma_start(out=dst, in_=src).then_inc(sW.h, 16)
        sW.inc(64)
        k.gp(b0_gp)
        def b0_sy(s):
            for dst, src in ((PSW[:, :], dpsw), (IDF[:, :], didf), (IDB[:, :], didb),
                             (S127[:, :], ds127), (SEL4[:, :], dsel4), (DTBC[:, :], ddtb),
                             (ALGC[:, :], dalog), (GNWC[:, :], dgnw),
                             (SENDM[:, :], dsendm), (RECVM[:, :], drecvm)):
                s.dma_start(out=dst, in_=src).then_inc(sX.h, 16)
        sX.inc(160)
        k.sy(b0_sy)
        wT0, xT0 = sW.n, sX.n
        def b0_dv(v):
            v.memset(O1F[:, :], 1.0)
            v.memset(OCB[:, :], 1.0)
            v.memset(ZR32[:, :], 0.0)
            v.wait_ge(sX.h, xT0)
            v.wait_ge(sW.h, wT0)
        k.dv(b0_dv)
        def b0_ac(sc):
            sc.wait_ge(sX.h, xT0)
            sc.activation(ACOL[:, :], ALGC[:, :], AF.Exp).then_inc(sA.h, 1)
        sA.inc()
        k.ac(b0_ac)
        k.pe(lambda t: (t.wait_ge(sW.h, wT0), t.wait_ge(sX.h, xT0)))
        k.emit()

        k = Blk(nc)
        aT0 = sA.n
        k.dv(lambda v: (v.wait_ge(sA.h, aT0),
                        v.tensor_scalar_mul(ACOL[:, :], ACOL[:, :], -1.0)))
        k.emit()

        def emit_ln(src, dst, tag, fold=False, srcw=None):
            with (
                nc.psum_tensor("pl1" + tag, [1, 1024], F32) as ps1,
                nc.psum_tensor("pl2" + tag, [1, 1024], F32) as ps2,
                nc.psum_tensor("pb0" + tag, [128, 1024], F32) as pb0,
                nc.psum_tensor("pb1" + tag, [128, 1024], F32) as pb1,
            ):
                k = Blk(nc)
                v0, p0, aq0 = sV.n, sP.n, sA.n
                def ln_dv(v):
                    if srcw is not None:
                        v.wait_ge(sV.h, srcw + FIN)
                    for f in range(0, FIN, 2):
                        if f >= 2:
                            v.wait_ge(sP.h, p0 + f - 1)
                        v.tensor_mul(SQTa, src[:, f * 1024:(f + 1) * 1024],
                                     src[:, f * 1024:(f + 1) * 1024]).then_inc(sV.h, 1)
                sV.inc(FIN // 2)
                k.dv(ln_dv)
                def ln_sq_ac(sc):
                    if srcw is not None:
                        sc.wait_ge(sV.h, srcw + FIN)
                    for f in range(1, FIN, 2):
                        if f >= 2:
                            sc.wait_ge(sP.h, p0 + f - 1)
                        sc.square(SQTb, src[:, f * 1024:(f + 1) * 1024]
                                  ).then_inc(sA.h, 1)
                sA.inc(FIN // 2)
                k.ac(ln_sq_ac)
                def ln_pe(t):
                    if srcw is not None:
                        t.wait_ge(sV.h, srcw + FIN)
                    for f in range(FIN):
                        for hh in range(2):
                            t.matmul(ps1[:, hh * 512:(hh + 1) * 512], OCB[:, :],
                                     src[:, f * 1024 + hh * 512:f * 1024 + (hh + 1) * 512],
                                     start=(f == 0), stop=(f == FIN - 1))
                    for f in range(FIN):
                        buf = SQTa if f % 2 == 0 else SQTb
                        if f % 2 == 0:
                            t.wait_ge(sV.h, v0 + f // 2 + 1)
                        else:
                            t.wait_ge(sA.h, aq0 + f // 2 + 1)
                        t.matmul(ps2[:, 0:512], OCB[:, :], buf[:, 0:512],
                                 start=(f == 0), stop=(f == FIN - 1))
                        t.matmul(ps2[:, 512:1024], OCB[:, :], buf[:, 512:1024],
                                 start=(f == 0), stop=(f == FIN - 1)).then_inc(sP.h, 1)
                sP.inc(FIN)
                k.pe(ln_pe)
                pT = sP.n
                def st_dv(v):
                    v.wait_ge(sP.h, pT)
                    v.tensor_scalar_mul(ROWA, ps1[:, :], 1.0 / HID)
                    v.tensor_mul(ROWB, ROWA, ROWA)
                    v.scalar_tensor_tensor(ROWB, ps2[:, :], 1.0 / HID, ROWB,
                                           ALU.mult, ALU.subtract)
                    v.tensor_scalar_add(ROWB, ROWB, EPS).then_inc(sV.h, 1)
                sV.inc()
                k.dv(st_dv)
                vT = sV.n
                def st_ac(sc):
                    sc.wait_ge(sV.h, vT)
                    sc.activation(ROWB, ROWB, AF.Ln)
                    sc.activation(ROWB, ROWB, AF.Exp, scale=-0.5).then_inc(sA.h, 1)
                sA.inc()
                k.ac(st_ac)
                aT = sA.n
                def st_dv2(v):
                    v.wait_ge(sA.h, aT)
                    v.scalar_tensor_tensor(ROWA, ROWA, -1.0, ROWB,
                                           ALU.mult, ALU.mult).then_inc(sV.h, 1)
                sV.inc()
                k.dv(st_dv2)
                vT2 = sV.n
                def st_pe(t):
                    t.wait_ge(sV.h, vT2)
                    for hh in range(2):
                        t.matmul(pb0[:, hh * 512:(hh + 1) * 512], O1F[:, :],
                                 ROWA[:, hh * 512:(hh + 1) * 512], start=True, stop=True)
                        ins = t.matmul(pb1[:, hh * 512:(hh + 1) * 512], O1F[:, :],
                                 ROWB[:, hh * 512:(hh + 1) * 512], start=True, stop=True)
                    ins.then_inc(sP.h, 1)
                sP.inc()
                k.pe(st_pe)
                pT2 = sP.n
                appB = sV.n + 1
                def app_dv(v, fold=fold):
                    v.wait_ge(sP.h, pT2)
                    v.tensor_copy(C0S, pb0[:, :])
                    v.tensor_copy(C1S, pb1[:, :]).then_inc(sV.h, 1)
                    if not fold:
                        for f in range(FIN):
                            v.tensor_tensor(dst[:, f * 1024:(f + 1) * 1024],
                                            src[:, f * 1024:(f + 1) * 1024], C1S, ALU.mult)
                            v.tensor_tensor(dst[:, f * 1024:(f + 1) * 1024],
                                            dst[:, f * 1024:(f + 1) * 1024], C0S,
                                            ALU.add).then_inc(sV.h, 1)
                sV.inc(1 if fold else 1 + FIN)
                k.dv(app_dv)
                k.emit(bar=fold)
                return appB

        def emit_ffn(xln, resid, tag, final=False, lncorr=False):
            with (
                nc.psum_tensor("pfh" + tag, [128, 1024], F32) as pha,
                nc.psum_tensor("pfl" + tag, [8, 1024], F32) as plg,
                nc.psum_tensor("pff" + tag, [128, 8], F32) as ptf,
                nc.psum_tensor("pfb" + tag, [128, 128], BF16) as ptb,
                nc.psum_tensor("pfo" + tag, [128, 1024], F32) as pbo,
            ):
                k = Blk(nc)
                w0 = sW.n
                def fw_gp(g):
                    g.dma_start(out=WSA[:, :], in_=daall).then_inc(sW.h, 16)
                    g.dma_start(out=WSB[:, 0:256], in_=droute).then_inc(sW.h, 16)
                sW.inc(32)
                k.gp(fw_gp)
                wT = sW.n
                def fh_pe(t, lncorr=lncorr):
                    t.wait_ge(sW.h, wT)
                    if lncorr:
                        for f in range(FIN):
                            t.matmul(ptf[:, 0:1], WSA[:, f * 128:(f + 1) * 128],
                                     OCB[:, :], start=(f == 0), stop=(f == FIN - 1))
                        for f in range(FIN):
                            t.matmul(ptf[0:8, 1:2], WSB[:, f * 8:(f + 1) * 8],
                                     OCB[:, :], start=(f == 0), stop=(f == FIN - 1))
                    for f in range(FIN):
                        for hh in range(2):
                            t.matmul(pha[:, hh * 512:(hh + 1) * 512],
                                     WSA[:, f * 128:(f + 1) * 128],
                                     xln[:, f * 1024 + hh * 512:f * 1024 + (hh + 1) * 512],
                                     start=(f == 0), stop=(f == FIN - 1))
                    for f in range(FIN):
                        for hh in range(2):
                            ins = t.matmul(plg[:, hh * 512:(hh + 1) * 512],
                                     WSB[:, f * 8:(f + 1) * 8],
                                     xln[:, f * 1024 + hh * 512:f * 1024 + (hh + 1) * 512],
                                     start=(f == 0), stop=(f == FIN - 1))
                    ins.then_inc(sP.h, 1)
                sP.inc()
                k.pe(fh_pe)
                pT = sP.n
                CSA = SCR[0:128, 5118:5119]
                CSR = SCR[0:8, 5117:5118]
                LTMP = SCR[0:8, 4096:5117]
                def fe_dv(v, lncorr=lncorr):
                    v.wait_ge(sP.h, pT)
                    if lncorr:
                        v.tensor_copy(CSA, ptf[:, 0:1])
                        v.tensor_copy(CSR, ptf[0:8, 1:2])
                        v.tensor_tensor(SQTb, pha[:, :], C1S, ALU.mult)
                        v.scalar_tensor_tensor(SQTa, C0S, CSA, SQTb, ALU.mult, ALU.add)
                        v.tensor_tensor(LTMP[:, 0:1021] if False else LG8, plg[:, :],
                                        C1S[0:8, :], ALU.mult)
                        ins = v.scalar_tensor_tensor(LG8, C0S[0:8, :].bitcast(F32)
                                                     if False else C0S[0:8, :],
                                                     CSR, LG8, ALU.mult, ALU.add)
                        ins.then_inc(sV.h, 1)
                    else:
                        v.tensor_copy(SQTa, pha[:, :])
                        v.tensor_copy(LG8, plg[:, :]).then_inc(sV.h, 1)
                sV.inc()
                k.dv(fe_dv)
                v0 = sV.n
                p1 = sP.n
                def fr_pe(t):
                    t.wait_ge(sV.h, v0)
                    for c in range(NC8):
                        if c > 0:
                            t.wait_ge(sV.h, v0 + c)
                        t.transpose(ptf[:, :], LG8[:, c * 128:(c + 1) * 128],
                                    IDF[0:8, 0:8]).then_inc(sP.h, 1)
                sP.inc(NC8)
                k.pe(fr_pe)
                def fr_dv(v):
                    for c in range(NC8):
                        v.wait_ge(sP.h, p1 + c + 1)
                        v.tensor_reduce(MXC, ptf[:, :], mybir.AxisListType.X, ALU.max)
                        v.tensor_scalar(DEC4[:, c * 8:(c + 1) * 8], ptf[:, :],
                                        MXC, MXC, ALU.is_equal, ALU.mult).then_inc(sV.h, 1)
                sV.inc(NC8)
                k.dv(fr_dv)

                v1 = sV.n
                p2 = sP.n
                def fw_pe(t, v0=v0):
                    t.wait_ge(sV.h, v0)
                    for c in range(NC8):
                        if c > 0:
                            t.wait_ge(sV.h, v1 + 2 * c)
                        t.transpose(ptb[:, :], SQTa[:, c * 128:(c + 1) * 128],
                                    IDB[:, :]).then_inc(sP.h, 1)
                        t.wait_ge(sV.h, v1 + 2 * c + 1)
                        t.transpose(ptb[:, :], WHT, IDB[:, :]).then_inc(sP.h, 1)
                sP.inc(2 * NC8)
                k.pe(fw_pe)
                def fw_dv(v):
                    for c in range(NC8):
                        v.wait_ge(sP.h, p2 + 2 * c + 1)
                        wsel3 = DEC4[:, c * 8:(c + 1) * 8].unsqueeze(2).broadcast_to([128, 8, 16])
                        v.tensor_tensor(WHT.rearrange("p (e r) -> p e r", e=8),
                                        ptb[:, :].rearrange("p (e r) -> p e r", e=8),
                                        wsel3, ALU.mult).then_inc(sV.h, 1)
                        v.wait_ge(sP.h, p2 + 2 * c + 2)
                        v.tensor_copy(WHF[:, c * 128:(c + 1) * 128], ptb[:, :]).then_inc(sV.h, 1)
                sV.inc(2 * NC8)
                k.dv(fw_dv)

                def fb_gp(g, pT=pT):
                    g.wait_ge(sP.h, pT)
                    g.dma_start(out=WSA[:, :], in_=dbw).then_inc(sW.h, 16)
                sW.inc(16)
                k.gp(fb_gp)
                wT2 = sW.n
                def fb_dv0(v):
                    v.wait_ge(sW.h, wT2)
                    v.tensor_scalar_mul(WSA[:, :], WSA[:, :], 0.5).then_inc(sV.h, 1)
                sV.inc()
                k.dv(fb_dv0)
                v2 = sV.n
                p3 = sP.n
                stslots = [sSL[8 + ot % 2].inc(16) for ot in range(FIN)] if final else None
                def fb_pe(t):
                    t.wait_ge(sV.h, v2)
                    for ot in range(FIN):
                        if ot > 1:
                            t.wait_ge(sV.h, v2 + ot - 1)
                        ps = pbo if ot % 2 == 0 else pha
                        for hh in range(2):
                            ins = t.matmul(ps[:, hh * 512:(hh + 1) * 512],
                                     WSA[:, ot * 128:(ot + 1) * 128],
                                     WHF[:, hh * 512:(hh + 1) * 512],
                                     start=True, stop=True)
                        ins.then_inc(sP.h, 1)
                sP.inc(FIN)
                k.pe(fb_pe)
                def fb_dv(v):
                    for ot in range(FIN):
                        v.wait_ge(sP.h, p3 + ot + 1)
                        ps = pbo if ot % 2 == 0 else pha
                        if final:
                            buf = OTMPa if ot % 2 == 0 else OTMPb
                            if ot >= 2:
                                v.wait_ge(sSL[8 + ot % 2].h, stslots[ot - 2])
                            v.tensor_tensor(buf, ps[:, :],
                                            resid[:, ot * 1024:(ot + 1) * 1024], ALU.add)
                            v.drain().then_inc(sV.h, 1)
                        else:
                            v.tensor_tensor(resid[:, ot * 1024:(ot + 1) * 1024], ps[:, :],
                                            resid[:, ot * 1024:(ot + 1) * 1024],
                                            ALU.add).then_inc(sV.h, 1)
                sV.inc(FIN)
                k.dv(fb_dv)
                if final:
                    def fb_sy(s):
                        for ot in range(FIN):
                            buf = OTMPa if ot % 2 == 0 else OTMPb
                            s.wait_ge(sV.h, v2 + 1 + ot)
                            s.dma_start(out=dout[ot * 128:(ot + 1) * 128, :],
                                        in_=buf).then_inc(sSL[8 + ot % 2].h, 16)
                    k.sy(fb_sy)
                k.emit()

        emit_ln(R1, None, "l1", fold=True)
        emit_ffn(R1, R1, "f1", lncorr=True)
        dbg_tap("x1", R1[:, 0:2048])
        app2 = emit_ln(R1, R2, "l2")
        dbg_tap("xln2", R2[:, 0:2048])

        k = Blk(nc)
        parkQ23 = sX.n + 32
        constT = sX.n + 64 + 96
        def park_sy(s, parkQ23=parkQ23):
            for i in (2, 3):
                src = R1[:, i * 8192:(i + 1) * 8192].rearrange("p (f t) -> p f t", f=8)
                dst = xpark[i * 1024:(i + 1) * 1024, :].rearrange("(f p) t -> p f t", p=128)
                s.dma_start(out=dst, in_=src).then_inc(sX.h, 16)
            s.wait_ge(sX.h, parkQ23)
            s.wait_ge(sV.h, app2 + FIN)
            for i in (0, 1):
                src = R1[:, i * 8192:(i + 1) * 8192].rearrange("p (f t) -> p f t", f=8)
                dst = xpark[i * 1024:(i + 1) * 1024, :].rearrange("(f p) t -> p f t", p=128)
                s.dma_start(out=dst, in_=src).then_inc(sX.h, 16)
            for dst, src in ((COSB, dcosf), (SINM, dsinf), (EYE, deye), (MSKN, dmskn),
                             (CTT, dcost), (STT, dsint)):
                s.dma_start(out=dst, in_=src).then_inc(sX.h, 16)
        sX.inc(160)
        k.sy(park_sy)
        k.emit(bar=False)

        with nc.psum_tensor("pz", [32, 1024], F32) as pz:
            k = Blk(nc)
            def z_gp(g):
                g.dma_start(out=WSB[:, 0:1024], in_=dwin).then_inc(sW.h, 16)
            sW.inc(16)
            k.gp(z_gp)
            wT = sW.n
            def z_pe(t, wT=wT, app2=app2):
                t.wait_ge(sW.h, wT)
                for f in range(FIN):
                    t.wait_ge(sV.h, app2 + f + 1)
                    for hh in range(2):
                        ins = t.matmul(pz[:, hh * 512:(hh + 1) * 512], WSB[:, f * 32:(f + 1) * 32],
                                 R2[:, f * 1024 + hh * 512:f * 1024 + (hh + 1) * 512],
                                 start=(f == 0), stop=(f == FIN - 1))
                ins.then_inc(sP.h, 1)
            sP.inc()
            k.pe(z_pe)
            zPeT = pT = sP.n
            def z_ac(sc, pT=pT):
                sc.wait_ge(sP.h, pT)
                sc.activation(DTF, pz[:, :], AF.Relu, bias=DTBC[:, :])
                sc.activation(T1, pz[:, :], AF.Abs, bias=DTBC[:, :])
                sc.activation(GFT, T1, AF.Exp, scale=-1.0)
                sc.activation(T1, GFT, AF.Ln, bias=1.0).then_inc(sA.h, 1)
            sA.inc()
            k.ac(z_ac)
            zAcT = aT = sA.n
            def z_dv(v, aT=aT):
                v.wait_ge(sA.h, aT)
                v.tensor_add(DTF, DTF, T1)
                v.tensor_scalar_mul(GFT, DTF, ACOL[:, :])
                for c in range(NC8):
                    v.tensor_tensor_scan(GFT[:, c * 128:(c + 1) * 128],
                                         GFT[:, c * 128:(c + 1) * 128],
                                         ZR32[:, :], 0.0, ALU.add, ALU.add)
                v.drain().then_inc(sV.h, 1)
            sV.inc()
            k.dv(z_dv)
            vZ = sV.n
            vT = sV.n
            def e2_ac(sc, vT=vT):
                sc.wait_ge(sV.h, vT)
                for c in range(NC8):
                    sc.activation(T1[:, c * 128:(c + 1) * 128],
                                  GFT[:, c * 128:(c + 1) * 128], AF.Exp, scale=-1.0,
                                  bias=GFT[:, c * 128 + 127:c * 128 + 128])
                sc.drain().then_inc(sA.h, 1)
            sA.inc()
            k.ac(e2_ac)
            aT2 = sA.n
            def e2_dv(v, aT2=aT2):
                v.wait_ge(sA.h, aT2)
                v.tensor_mul(T1, T1, DTF).then_inc(sV.h, 1)
            sV.inc()
            k.dv(e2_dv)
            vE = sV.n
            def e2_ac2(sc, vE=vE):
                sc.wait_ge(sV.h, vE)
                sc.activation(DTF, DTF, AF.Ln).then_inc(sA.h, 1)
            sA.inc()
            k.ac(e2_ac2)
            aE = sA.n
            def e2_dv2(v, aE=aE):
                v.wait_ge(sA.h, aE)
                v.tensor_sub(DTF, DTF, GFT).then_inc(sV.h, 1)
            sV.inc()
            k.dv(e2_dv2)
            vT2 = sV.n
            k.emit(bar=False)

        dbg_tap("gft", SCR[0:32, 1024:2048])

        kvEvac = []
        with nc.psum_tensor("pkv", [128, 4096], F32) as pkv:
            k = Blk(nc)
            for q in range(4):
                w0 = sW.n
                p0 = sP.n
                slot_tgts = []
                for f in range(FIN):
                    slot_tgts.append(sSL[f % 8].inc(16))
                def kv_gp(g, q=q, p0=p0, slot_tgts=slot_tgts):
                    for f in range(FIN):
                        if f >= 8:
                            g.wait_ge(sP.h, p0 + f - 7)
                        elif q > 0:
                            g.wait_ge(sP.h, p0)
                        base = (q * 32 + f) * 128
                        g.dma_start(out=WSA[:, (f % 8) * 512:(f % 8 + 1) * 512],
                                    in_=dwvk[base:base + 128, :]).then_inc(sSL[f % 8].h, 16)
                k.gp(kv_gp)
                vPrevEv = kvEvac[-1] if kvEvac else None
                def kv_pe(t, q=q, slot_tgts=slot_tgts, vPrevEv=vPrevEv,
                          zAcT=zAcT):
                    if q == 0:
                        t.wait_ge(sA.h, zAcT)
                    else:
                        t.wait_ge(sV.h, vPrevEv)
                    for f in range(FIN):
                        t.wait_ge(sSL[f % 8].h, slot_tgts[f])
                        if q == 0:
                            t.wait_ge(sV.h, app2 + f + 1)
                        for c in range(NC8):
                            ins = t.matmul(pkv[:, c * 512:(c + 1) * 512],
                                     R2[:, f * 1024 + c * 128:f * 1024 + (c + 1) * 128],
                                     WSA[:, (f % 8) * 512:(f % 8 + 1) * 512],
                                     start=(f == 0), stop=(f == FIN - 1))
                        ins.then_inc(sP.h, 1)
                sP.inc(FIN)
                k.pe(kv_pe)
                pT = sP.n
                def kv_dv(v, q=q, pT=pT):
                    v.wait_ge(sP.h, pT)
                    for c in range(NC8):
                        if q < 2:
                            v.tensor_copy(VTB[:, c * 1024 + q * 512:c * 1024 + (q + 1) * 512],
                                          pkv[:, c * 512:(c + 1) * 512])
                        else:
                            v.tensor_copy(
                                KTB[:, c * 1024 + (q - 2) * 512:c * 1024 + (q - 1) * 512],
                                pkv[:, c * 512:(c + 1) * 512])
                    v.drain().then_inc(sV.h, 1)
                sV.inc()
                kvEvac.append(sV.n)
                k.dv(kv_dv)
            k.emit(bar=False)
        vEvac3 = kvEvac[-1]

        with (
            nc.psum_tensor("ptr", [128, 32], F32) as ptr,
            nc.psum_tensor("pcd", [128, 32], F32) as pcd,
        ):
            k = Blk(nc)
            p3b = sP.n
            a3b = sA.n
            trDvB = sV.n
            def tr_pe(t, trDvB=trDvB, vT2=vT2, vEvac3=vEvac3):
                t.wait_ge(sV.h, vT2)
                t.wait_ge(sV.h, vEvac3)
                for c in range(NC8):
                    t.transpose(ptr[:, :], GFT[:, c * 128:(c + 1) * 128],
                                IDF[0:32, 0:32]).then_inc(sP.h, 1)
                    t.wait_ge(sV.h, trDvB + 3 * c + 1)
                    t.transpose(ptr[:, :], DTF[:, c * 128:(c + 1) * 128],
                                IDF[0:32, 0:32]).then_inc(sP.h, 1)
                    t.wait_ge(sV.h, trDvB + 3 * c + 2)
                    t.transpose(ptr[:, :], T1[:, c * 128:(c + 1) * 128],
                                IDF[0:32, 0:32]).then_inc(sP.h, 1)
                    t.wait_ge(sV.h, trDvB + 3 * c + 3)
                    if c > 0:
                        t.wait_ge(sA.h, a3b + c)
                    t.matmul(pcd[:, :], S127[:, :], NGT[:, c * 32:(c + 1) * 32],
                             start=True, stop=True).then_inc(sP.h, 1)
            sP.inc(4 * NC8)
            k.pe(tr_pe)
            def tr_dv(v, p3b=p3b):
                for c in range(NC8):
                    v.wait_ge(sP.h, p3b + 4 * c + 1)
                    v.tensor_scalar_mul(NGT[:, c * 32:(c + 1) * 32], ptr[:, :], -1.0)
                    v.drain().then_inc(sV.h, 1)
                    v.wait_ge(sP.h, p3b + 4 * c + 2)
                    v.tensor_copy(DTTc[:, c * 32:(c + 1) * 32], ptr[:, :])
                    v.drain().then_inc(sV.h, 1)
                    v.wait_ge(sP.h, p3b + 4 * c + 3)
                    v.tensor_copy(EDT[:, c * 32:(c + 1) * 32], ptr[:, :])
                    v.drain().then_inc(sV.h, 1)
            sV.inc(3 * NC8)
            k.dv(tr_dv)
            def tr_ac(sc, p3b=p3b):
                for c in range(NC8):
                    sc.wait_ge(sP.h, p3b + 4 * c + 4)
                    sc.activation(CDB[:, c * 32:(c + 1) * 32], pcd[:, :], AF.Exp,
                                  scale=-1.0)
                    sc.drain().then_inc(sA.h, 1)
            sA.inc(NC8)
            k.ac(tr_ac)
            k.emit(bar=False)
        vTRtot = sV.n
        aTRtot = sA.n
        trT = sP.n

        with (
            nc.psum_tensor("pqa", [128, 1024], F32) as pqa,
            nc.psum_tensor("prt", [128, 1024], F32) as prt,
            nc.psum_tensor("pst", [128, 1024], F32) as pst,
            nc.psum_tensor("py4", [128, 512], F32) as py4,
            nc.psum_tensor("pc4", [128, 512], F32) as pc4,
        ):
            k = Blk(nc)
            k.dv(lambda v: (v.wait_ge(sX.h, constT), v.wait_ge(sP.h, trT),
                            v.memset(SSQ, 0.0)))
            k.emit(bar=False)

            xCRB = {}
            xPark = {}
            slabT = {}
            prevT = None

            A4P = [A4S, A4SB, A4SC, A4SD]
            def A4Sp(c):
                return A4P[c % 4]

            SLOT = [0, 1, 2, 0, 1, 2, 0, 2]
            prevslot = {}
            last = {}
            for _c in range(NC8):
                prevslot[_c] = last.get(SLOT[_c])
                last[SLOT[_c]] = _c

            def SBK(c):
                s = SLOT[c]
                if s == 0:
                    return pst[:, 0:512], pst[:, 512:1024]
                if s == 1:
                    return pqa[:, 0:512], pqa[:, 512:1024]
                return py4[:, :], pc4[:, :]

            def PRVp(j):
                return PRV if j % 2 == 0 else PRV2

            def SM4p(c):
                return SM4A if c % 2 == 0 else SM4B

            def KD4p(c):
                return KD4A if c % 2 == 0 else KD4B

            def YGRp(gg):
                return YGR if gg % 2 == 0 else YGRB

            k = Blk(nc)
            for g in range(NG + 1):
                has_a = g < NG
                has_p2 = g > 0
                pS, vS, aS, gS = sP.n, sV.n, sA.n, sG.n
                if has_p2:
                    xCRB[g - 1] = sW.inc(16)
                if has_a:
                    xin = sX.inc(16)

                peo = []
                if has_a:
                    peo += [("qp", 0), ("qp", 1)]
                    if has_p2:
                        peo += [("inter", c) for c in range(4)]
                    peo += [("rot", 0), ("qp", 2)]
                    if has_p2:
                        peo += [("inter", c) for c in range(4, NC8)]
                        peo.append(("ssq", 0))
                    peo += [("rot", 1), ("qp", 3)]
                    if has_p2:
                        peo.append(("ssq", 1))
                    peo += [("rot", 2), ("rot", 3)]
                    if has_p2:
                        peo += [("ssq", 2), ("ssq", 3)]
                    peo += [("stgb", 0), ("stgb", 1), ("stgb", 2)]
                    for c in range(NC8):
                        peo += [("ymm", c), ("cmm", c)]
                        if c + 3 < NC8:
                            peo.append(("stgb", c + 3))
                else:
                    peo += [("inter", c) for c in range(NC8)]
                    peo += [("ssq", hh) for hh in range(HG)]
                ptg = {}
                pa_tg = {}
                pi, pai = sP.n, sPa.n
                for op in peo:
                    if op[0] == "stgb":
                        pai += 1
                        pa_tg[op] = pai
                    else:
                        pi += 1
                        ptg[op] = pi

                dveo = []
                if g == 0:
                    dveo += [("krope", c) for c in range(NC8)]
                if has_p2:
                    dveo.append(("prvi", 0))
                    for j in range(NC8):
                        if j == 6 and has_a:
                            dveo.append(("evac", 0))
                        dveo.append(("pvb", j))
                        if j < NC8 - 1:
                            dveo.append(("scan", j))
                        if 2 <= j:
                            dveo.append(("yfinal", j - 2))
                    dveo += [("yfinal", NC8 - 2), ("yfinal", NC8 - 1)]
                if has_a:
                    if not has_p2:
                        dveo.append(("evac", 0))
                    dveo += [("o2", 0), ("qg", 0)]
                    for hh in range(1, HG):
                        dveo += [("evac", hh), ("o2", hh), ("qg", hh)]
                    dveo += [("a4", 0), ("a4", 1)]
                    if has_p2:
                        dveo.append(("sacc", 0))
                    dveo.append(("a4", 2))
                    for c in range(NC8):
                        dveo += [("stm", c), ("kdec", c), ("qs", c)]
                        if c >= 2:
                            dveo.append(("carry", c - 2))
                        for x in range(3, NC8):
                            if prevslot[x] == c:
                                dveo.append(("a4", x))
                    dveo += [("carry", NC8 - 2), ("carry", NC8 - 1), ("cms", 0)]
                elif has_p2:
                    dveo.append(("sacc", 0))
                vtg = {op: vS + i + 1 for i, op in enumerate(dveo)}

                aco = []
                if has_p2:
                    aco += [("ysq", hh) for hh in range(HG)]
                if has_a:
                    aco += [("e1b", 0), ("dec", 0), ("e1b", 1), ("dec", 1),
                            ("e1b", 2), ("dec", 2)]
                    for c in range(NC8):
                        aco += [("yie", c), ("cst", c)]
                        if c + 3 < NC8:
                            aco += [("e1b", c + 3), ("dec", c + 3)]
                atg = {op: aS + i + 1 for i, op in enumerate(aco)}
                g_qs = [gS + c + 1 for c in range(NC8)]

                if has_p2:
                    def g_crb(gp, g=g):
                        gp.wait_ge(sC.h, g)
                        gp.dma_start(out=CRB, in_=ccout.ap()).then_inc(sW.h, 16)
                    k.gp(g_crb)
                if has_a:
                    kf_prev = sSL[10].n
                    kf_tot = sSL[10].inc(16 * 9)

                    def p1_gpq(gp, g=g, xin=xin, has_p2=has_p2):
                        gp.wait_ge(sX.h, xin)
                        if has_p2:
                            gp.wait_ge(sW.h, xCRB[g - 1])
                        gp.collective_compute(
                            "AllReduce", ALU.add,
                            replica_groups=[[0, 1], [2, 3], [4, 5], [6, 7]],
                            ins=[ccin.ap().opt()],
                            outs=[ccout.ap().opt()],
                        ).then_inc(sC.h, 1)
                    sC.inc()
                    k.gp(p1_gpq)

                def mk_slab(gg, hh, wait_qp, wait_abs=None):
                    sem = sQW[hh % 2]
                    pv = sem.n
                    tgt = sem.inc(16)
                    slabT[(gg, hh)] = tgt

                    def f(s, gg=gg, hh=hh, wait_qp=wait_qp, pv=pv, sem=sem,
                          ptg=ptg, wait_abs=wait_abs):
                        if wait_abs is not None:
                            s.wait_ge(sP.h, wait_abs)
                        if wait_qp is not None:
                            s.wait_ge(sP.h, ptg[wait_qp])
                        s.wait_ge(sem.h, pv)
                        hgl = gg * HG + hh
                        buf = WSA if hh % 2 == 0 else WSB
                        s.dma_start(out=buf[:, :],
                                    in_=dwq[hgl * 128:(hgl + 1) * 128, :]
                                    ).then_inc(sem.h, 16)
                    return f

                if has_a:
                    if g == 0:
                        k.sy(mk_slab(0, 0, None, wait_abs=trT))
                        k.sy(mk_slab(0, 1, None, wait_abs=trT))

                    vKR = vtg[("krope", NC8 - 1)] if g == 0 else None

                    def a_sy1(s, g=g, kf_prev=kf_prev, vKR=vKR):
                        if g == 0:
                            s.wait_ge(sV.h, vZ)
                        s.wait_ge(sSL[10].h, kf_prev)
                        s.dma_start(out=GR4, in_=GFT[g * HG:(g + 1) * HG, :]
                                    ).then_inc(sSL[10].h, 16)
                        if g == 0:
                            s.wait_ge(sV.h, vKR)
                        for c in range(NC8):
                            s.dma_start_transpose(
                                KFG[:, c * 128:(c + 1) * 128],
                                KTB[:, c * 1024 + g * 128:c * 1024 + (g + 1) * 128],
                            ).then_inc(sSL[10].h, 16)
                    k.sy(a_sy1)
                    k.sy(mk_slab(g, 2, ("qp", 0)))
                    k.sy(mk_slab(g, 3, ("qp", 1)))
                if has_p2:
                    pk_prev = sSL[11].n
                    yfin_all = vtg[("yfinal", NC8 - 1)]

                    def p2_sy2(s, g=g, yfin_all=yfin_all, pk_prev=pk_prev):
                        s.wait_ge(sV.h, yfin_all)
                        s.wait_ge(sSL[11].h, pk_prev)
                        s.dma_start(out=ypark[g - 1, :, :], in_=YGRp(g - 1)
                                    ).then_inc(sSL[11].h, 16)
                    xPark[g - 1] = sSL[11].inc(16)
                    k.sy(p2_sy2)
                if has_a:
                    if g + 1 < NG:
                        k.sy(mk_slab(g + 1, 0, ("qp", 2)))
                        k.sy(mk_slab(g + 1, 1, ("qp", 3)))
                    vcm = vtg[("cms", 0)]

                    def p1_sy(s, vcm=vcm, g=g):
                        s.wait_ge(sV.h, vcm)
                        s.wait_ge(sC.h, g)
                        s.dma_start(out=ccin.ap(), in_=CMS).then_inc(sX.h, 16)
                    k.sy(p1_sy)

                def s_pe(t, g=g, peo=peo, ptg=ptg, pa_tg=pa_tg, vtg=vtg, atg=atg,
                         prevT=prevT, g_qs=g_qs,
                         kf_tot=kf_tot if has_a else None):
                    for op in peo:
                        nm, c = op
                        if nm == "qp":
                            t.wait_ge(sQW[c % 2].h, slabT[(g, c)])
                            if c == 0 and g == 0:
                                t.wait_ge(sV.h, vTRtot)
                                t.wait_ge(sA.h, aTRtot)
                            if c == 0 and prevT is not None:
                                t.wait_ge(sV.h, prevT["vtg"][("qs", 4)])
                            if c == 1 and prevT is not None:
                                t.wait_ge(sA.h, prevT["atg"][("cst", 7)])
                            if c >= 2:
                                t.wait_ge(sV.h, vtg[("evac", c - 2)])
                            buf = WSA if c % 2 == 0 else WSB
                            dst = pqa if c % 2 == 0 else prt
                            for f in range(FIN):
                                for px in range(2):
                                    ins = t.matmul(
                                        dst[:, px * 512:(px + 1) * 512],
                                        buf[:, f * 128:(f + 1) * 128],
                                        R2[:, f * 1024 + px * 512:f * 1024 + (px + 1) * 512],
                                        start=(f == 0), stop=(f == FIN - 1))
                            ins.then_inc(sP.h, 1)
                        elif nm == "rot":
                            t.wait_ge(sV.h, vtg[("evac", c)])
                            if c == 0 and prevT is not None:
                                t.wait_ge(sV.h, prevT["vtg"][("qs", 6)])
                            for px in range(2):
                                ins = t.matmul(pst[:, px * 512:(px + 1) * 512],
                                               PSW[:, :],
                                               QRW[:, px * 512:(px + 1) * 512],
                                               start=True, stop=True)
                            ins.then_inc(sP.h, 1)
                        elif nm == "inter":
                            t.wait_ge(sV.h, vtg[("pvb", c)])
                            t.wait_ge(sV.h, prevT["vtg"][("qs", c)])
                            if c > 1:
                                t.wait_ge(sV.h, vtg[("yfinal", c - 2)])
                            yb = py4 if c % 2 == 0 else pc4
                            pvs = PRVB2 if c % 2 == 0 else PRVB2B
                            for hh in range(HG):
                                ins = t.matmul(yb[:, hh * 128:(hh + 1) * 128],
                                               pvs[:, hh * 128:(hh + 1) * 128],
                                               QS[:, c * 512 + hh * 128:c * 512 + (hh + 1) * 128],
                                               start=(hh == 0), stop=(hh == 3))
                            ins.then_inc(sP.h, 1)
                        elif nm == "stgb":
                            stb, gbb = SBK(c)
                            if c == 0:
                                t.wait_ge(sSL[10].h, kf_tot)
                                t.wait_ge(sV.h, vtg[("qg", HG - 1)])
                            elif c == 1:
                                t.wait_ge(sV.h, vtg[("evac", 2)])
                            elif c == 2:
                                if ("sacc", 0) in vtg:
                                    t.wait_ge(sV.h, vtg[("sacc", 0)])
                            else:
                                pc = prevslot[c]
                                t.wait_ge(sV.h, vtg[("qs", pc)])
                            t.matmul(stb, KFG[:, c * 128:(c + 1) * 128],
                                     QG[:, c * 512:(c + 1) * 512], start=True, stop=True)
                            for hh in range(HG):
                                ins = t.matmul(gbb[:, hh * 128:(hh + 1) * 128],
                                               SEL4[:, hh * 128:(hh + 1) * 128],
                                               GR4[:, c * 128:(c + 1) * 128],
                                               start=(hh == 0), stop=(hh == 3))
                            ins.then_inc(sPa.h, 1)
                        elif nm == "ymm":
                            t.wait_ge(sV.h, vtg[("stm", c)])
                            if c > 0:
                                t.wait_ge(sA.h, atg[("yie", c - 1)])
                            t.matmul(prt[:, 0:512],
                                     VTB[:, c * 1024 + g * 128:c * 1024 + (g + 1) * 128],
                                     SM4p(c), start=True, stop=True).then_inc(sP.h, 1)
                        elif nm == "cmm":
                            t.wait_ge(sV.h, vtg[("kdec", c)])
                            if c > 0:
                                t.wait_ge(sA.h, atg[("cst", c - 1)])
                            for hh in range(HG):
                                ins = t.matmul(prt[:, 512 + hh * 128:512 + (hh + 1) * 128],
                                               KD4p(c)[:, hh * 128:(hh + 1) * 128],
                                               VTB[:, c * 1024 + g * 128:c * 1024 + (g + 1) * 128],
                                               start=(hh == 0), stop=(hh == 3))
                            ins.then_inc(sP.h, 1)
                        else:
                            hh = c
                            t.wait_ge(sA.h, atg[("ysq", hh)])
                            hgl = (g - 1) * HG + hh
                            t.matmul(py4[0:32, :], EYE[:, hgl * 32:(hgl + 1) * 32],
                                     SQY2[:, 0:512],
                                     start=(hh == 0), stop=(hh == 3))
                            t.matmul(pc4[0:32, :], EYE[:, hgl * 32:(hgl + 1) * 32],
                                     SQY2[:, 512:1024],
                                     start=(hh == 0), stop=(hh == 3)).then_inc(sP.h, 1)
                if has_a:
                    sP.inc(2 * HG + 2 * NC8)
                    sPa.inc(NC8)
                if has_p2:
                    sP.inc(NC8 + HG)
                k.pe(s_pe)

                def s_dv(v, g=g, dveo=dveo, vtg=vtg, ptg=ptg, pa_tg=pa_tg,
                         atg=atg, g_qs=g_qs, gS=gS, prevT=prevT):
                    for op in dveo:
                        nm, c = op
                        if nm == "evac":
                            v.wait_ge(sP.h, ptg[("qp", c)])
                            src = pqa if c % 2 == 0 else prt
                            v.tensor_copy(QRW, src[:, :]).then_inc(sV.h, 1)
                        elif nm == "o2":
                            v.wait_ge(sP.h, ptg[("rot", c)])
                            v.tensor_tensor(QRW, QRW, COSB, ALU.mult)
                            v.tensor_tensor(O2S, pst[:, :], SINM, ALU.mult).then_inc(sV.h, 1)
                        elif nm == "qg":
                            qg3 = QG.rearrange("p (c h i) -> p c h i", c=8, h=4)[:, :, c, :]
                            v.tensor_tensor(qg3,
                                            QRW.rearrange("p (c i) -> p c i", c=8),
                                            O2S.rearrange("p (c i) -> p c i", c=8),
                                            ALU.add).then_inc(sV.h, 1)
                        elif nm == "krope":
                            kc = KTB[:, c * 1024:(c + 1) * 1024]
                            kc3 = kc.rearrange("p (kv d) -> p kv d", kv=8)
                            cos3 = CTT[:, c * 128:(c + 1) * 128].unsqueeze(1)\
                                .broadcast_to([128, 8, 128])
                            sin_lo = STT[:, c * 128:c * 128 + 64].unsqueeze(1)\
                                .broadcast_to([128, 8, 64])
                            sin_hi = STT[:, c * 128 + 64:(c + 1) * 128].unsqueeze(1)\
                                .broadcast_to([128, 8, 64])
                            kr13 = YGRB[:, 0:1024].rearrange("p (kv d) -> p kv d", kv=8)
                            kr23 = YGRB[:, 1024:2048].rearrange("p (kv d) -> p kv d", kv=8)
                            v.tensor_tensor(kr23[:, :, 0:64], kc3[:, :, 64:128],
                                            sin_lo, ALU.mult)
                            v.tensor_tensor(kr23[:, :, 64:128], kc3[:, :, 0:64],
                                            sin_hi, ALU.mult)
                            v.tensor_tensor(kr13, kc3, cos3, ALU.mult)
                            v.tensor_add(kc, YGRB[:, 0:1024],
                                         YGRB[:, 1024:2048]).then_inc(sV.h, 1)
                        elif nm == "prvi":
                            v.wait_ge(sW.h, xCRB[g - 1])
                            v.tensor_scalar_mul(PRV[:, :], CRB,
                                                RECVM[:, :]).then_inc(sV.h, 1)
                        elif nm == "pvb":
                            if c >= 2:
                                v.wait_ge(sP.h, ptg[("inter", c - 2)])
                            elif prevT is not None and prevT["had_p2"]:
                                v.wait_ge(sP.h, prevT["ptg"][("inter", NC8 - 2 + c)])
                            dstp = PRVB2 if c % 2 == 0 else PRVB2B
                            v.tensor_copy(dstp, PRVp(c)[:, :]).then_inc(sV.h, 1)
                        elif nm == "scan":
                            for hh in range(HG):
                                hgl = (g - 1) * HG + hh
                                ins = v.scalar_tensor_tensor(
                                    PRVp(c + 1)[:, hh * 128:(hh + 1) * 128],
                                    PRVp(c)[:, hh * 128:(hh + 1) * 128],
                                    CDB[:, c * 32 + hgl:c * 32 + hgl + 1],
                                    CST[:, c * 512 + hh * 128:c * 512 + (hh + 1) * 128],
                                    ALU.mult, ALU.add)
                            ins.then_inc(sV.h, 1)
                        elif nm == "yfinal":
                            v.wait_ge(sP.h, ptg[("inter", c)])
                            yb = py4 if c % 2 == 0 else pc4
                            yg3 = YGRp(g - 1).rearrange(
                                "p (h c i) -> p h c i", h=4, c=8)[:, :, c, :]
                            v.tensor_tensor(yg3,
                                            yb[:, :].rearrange("p (h i) -> p h i", h=4),
                                            yg3, ALU.add).then_inc(sV.h, 1)
                        elif nm == "a4":
                            gbb = SBK(c)[1]
                            v.wait_ge(sPa.h, pa_tg[("stgb", c)])
                            for hh in range(HG):
                                hgl = g * HG + hh
                                ins = v.scalar_tensor_tensor(
                                    A4Sp(c)[:, hh * 128:(hh + 1) * 128],
                                    gbb[:, hh * 128:(hh + 1) * 128],
                                    DTTc[:, c * 32 + hgl:c * 32 + hgl + 1],
                                    MSKN, ALU.add, ALU.add)
                            ins.then_inc(sV.h, 1)
                        elif nm == "stm":
                            stb = SBK(c)[0]
                            v.wait_ge(sA.h, atg[("dec", c)])
                            if c > 1:
                                v.wait_ge(sP.h, ptg[("ymm", c - 2)])
                            v.tensor_tensor(SM4p(c), stb, A4Sp(c),
                                            ALU.mult).then_inc(sV.h, 1)
                        elif nm == "kdec":
                            if c > 1:
                                v.wait_ge(sP.h, ptg[("cmm", c - 2)])
                            kin = KTB[:, c * 1024 + g * 128:c * 1024 + (g + 1) * 128]
                            v.tensor_tensor(
                                KD4p(c).rearrange("p (h d) -> p h d", h=4),
                                kin.unsqueeze(1).broadcast_to([128, 4, 128]),
                                EDT[:, c * 32 + g * 4:c * 32 + g * 4 + 4]
                                    .unsqueeze(2).broadcast_to([128, 4, 128]),
                                ALU.mult).then_inc(sV.h, 1)
                        elif nm == "carry":
                            v.wait_ge(sA.h, atg[("cst", c)])
                            if c == 0:
                                v.memset(CARRY[:, :], 0.0)
                            for hh in range(HG):
                                hgl = g * HG + hh
                                ins = v.scalar_tensor_tensor(
                                    CARRY[:, hh * 128:(hh + 1) * 128],
                                    CARRY[:, hh * 128:(hh + 1) * 128],
                                    CDB[:, c * 32 + hgl:c * 32 + hgl + 1],
                                    CST[:, c * 512 + hh * 128:c * 512 + (hh + 1) * 128],
                                    ALU.mult, ALU.add)
                            ins.then_inc(sV.h, 1)
                        elif nm == "qs":
                            v.wait_ge(sA.h, atg[("e1b", c)])
                            if ("inter", c) in ptg:
                                v.wait_ge(sP.h, ptg[("inter", c)])
                            v.tensor_tensor(QS[:, c * 512:(c + 1) * 512],
                                            QG[:, c * 512:(c + 1) * 512], SBK(c)[1],
                                            ALU.mult).then_inc(sV.h, 1)
                        elif nm == "cms":
                            v.tensor_scalar_mul(CMS, CARRY[:, :],
                                                SENDM[:, :]).then_inc(sV.h, 1)
                        else:
                            v.wait_ge(sP.h, ptg[("ssq", HG - 1)])
                            v.tensor_tensor(SSQ[:, 0:512], SSQ[:, 0:512],
                                            py4[0:32, :], ALU.add)
                            v.tensor_tensor(SSQ[:, 512:1024], SSQ[:, 512:1024],
                                            pc4[0:32, :], ALU.add).then_inc(sV.h, 1)
                sV.inc(len(dveo))
                k.dv(s_dv)

                def s_ac(sc, g=g, aco=aco, vtg=vtg, ptg=ptg, g_qs=g_qs,
                         has_p2=has_p2):
                    for nm, c in aco:
                        if nm == "ysq":
                            if c == 0:
                                sc.wait_ge(sV.h, vtg[("yfinal", NC8 - 1)])
                            else:
                                sc.wait_ge(sP.h, ptg[("ssq", c - 1)])
                            ys = YGRp(g - 1)
                            sc.square(SQY2, ys[:, c * 1024:(c + 1) * 1024]
                                      ).then_inc(sA.h, 1)
                        elif nm == "dec":
                            sc.activation(A4Sp(c), A4Sp(c), AF.Exp).then_inc(sA.h, 1)
                        elif nm == "e1b":
                            gbb = SBK(c)[1]
                            sc.wait_ge(sV.h, vtg[("a4", c)])
                            sc.activation(gbb, gbb, AF.Exp).then_inc(sA.h, 1)
                        elif nm == "yie":
                            sc.wait_ge(sP.h, ptg[("ymm", c)])
                            if c == 0 and g >= 2:
                                sc.wait_ge(sSL[11].h, xPark[g - 2])
                            yg3 = YGRp(g).rearrange(
                                "p (h c i) -> p h c i", h=4, c=8)[:, :, c, :]
                            sc.copy(yg3,
                                    prt[:, 0:512].rearrange("p (h i) -> p h i", h=4)
                                    ).then_inc(sA.h, 1)
                        else:
                            sc.wait_ge(sP.h, ptg[("cmm", c)])
                            if has_p2 and c < NC8 - 1:
                                sc.wait_ge(sV.h, vtg[("scan", c)])
                            sc.copy(CST[:, c * 512:(c + 1) * 512],
                                    prt[:, 512:1024]).then_inc(sA.h, 1)
                sA.inc(len(aco))
                k.ac(s_ac)

                prevT = dict(vtg=vtg, atg=atg, ptg=ptg, gqs=g_qs, had_p2=has_p2)
                if not has_a:
                    break
            k.emit()


        dbg_tap("ssq", SCR[0:32, 0:1024])

        with (
            nc.psum_tensor("pga", [128, 1024], F32) as pga,
            nc.psum_tensor("pgb", [128, 1024], F32) as pgb,
            nc.psum_tensor("prb", [128, 1024], F32) as prb,
        ):
            k = Blk(nc)
            vr = sV.n
            pk_all = sSL[11].n
            def r_dv(v, pk_all=pk_all):
                v.wait_ge(sSL[11].h, pk_all)
                v.tensor_scalar(RSTD, SSQ, 1.0 / D, RMS_EPS, ALU.mult, ALU.add)
                v.drain().then_inc(sV.h, 1)
            sV.inc()
            k.dv(r_dv)
            vrT = sV.n
            def r_ac(sc):
                sc.wait_ge(sV.h, vrT)
                sc.activation(RSTD, RSTD, AF.Ln)
                sc.activation(RSTD, RSTD, AF.Exp, scale=-0.5).then_inc(sA.h, 1)
            sA.inc()
            k.ac(r_ac)
            k.emit(bar=False)

            aR = sA.n
            k = Blk(nc)
            for g in range(NG):
                w0 = sW.n
                x0 = sX.n
                p0 = sP.n
                v0 = sV.n
                a0 = sA.n
                slabt = [sSL[hh % 2].inc(16) for hh in range(HG)]
                yht = [sSL[8 + hh % 2].inc(16) for hh in range(HG)]
                gr4t = sSL[10].inc(16)
                def g_gp(gp, g=g, p0=p0, slabt=slabt):
                    for hh in range(HG):
                        hgl = g * HG + hh
                        buf = WSA if hh % 2 == 0 else WSB
                        if hh >= 2:
                            gp.wait_ge(sP.h, p0 + (hh - 2) * 2 + 1)
                        elif g > 0:
                            gp.wait_ge(sP.h, p0 - 3 if hh == 0 else p0 - 1)
                        gp.dma_start(out=buf[:, :],
                                     in_=dwg[hgl * 128:(hgl + 1) * 128, :]
                                     ).then_inc(sSL[hh % 2].h, 16)
                k.gp(g_gp)

                yh_prev = (sSL[8].n - (32 if g > 0 else 0), sSL[9].n - (32 if g > 0 else 0))
                def g_sy(s, g=g, v0=v0, gr4t=gr4t, yht=yht, p0=p0, aR=aR):
                    if g == 0:
                        s.wait_ge(sA.h, aR)
                        s.wait_ge(sV.h, v0)
                    else:
                        s.wait_ge(sP.h, p0)
                    s.wait_ge(sSL[10].h, gr4t - 16)
                    s.dma_start(out=GR4, in_=RSTD[g * HG:(g + 1) * HG, :]
                                ).then_inc(sSL[10].h, 16)
                    for hh in range(HG):
                        if hh >= 2:
                            s.wait_ge(sV.h, v0 + (hh - 2) * 3 + 3)
                        else:
                            s.wait_ge(sSL[8 + hh].h, yht[hh] - 16)
                            if g > 0:
                                s.wait_ge(sV.h, v0 - 5 + 3 * hh)
                        s.dma_start(out=YHp[hh % 2],
                                    in_=ypark[g, :, hh * 1024:(hh + 1) * 1024]
                                    ).then_inc(sSL[8 + hh % 2].h, 16)
                k.sy(g_sy)

                def g_pe(t, g=g, v0=v0, a0=a0, slabt=slabt, gr4t=gr4t):
                    for hh in range(HG):
                        t.wait_ge(sSL[hh % 2].h, slabt[hh])
                        ps = pga if hh % 2 == 0 else pgb
                        buf = WSA if hh % 2 == 0 else WSB
                        if hh == 0:
                            t.wait_ge(sA.h, a0 - 1)
                        elif hh == 1:
                            t.wait_ge(sA.h, a0)
                        if hh >= 2:
                            t.wait_ge(sV.h, v0 + (hh - 2) * 3 + 3)
                        for f in range(FIN):
                            for p2 in range(2):
                                ins = t.matmul(ps[:, p2 * 512:(p2 + 1) * 512],
                                         buf[:, f * 128:(f + 1) * 128],
                                         R2[:, f * 1024 + p2 * 512:f * 1024 + (p2 + 1) * 512],
                                         start=(f == 0), stop=(f == FIN - 1))
                        ins.then_inc(sP.h, 1)
                        t.wait_ge(sSL[10].h, gr4t)
                        if hh > 0:
                            t.wait_ge(sV.h, v0 + (hh - 1) * 3 + 3)
                        else:
                            t.wait_ge(sV.h, v0)
                        for p2 in range(2):
                            ins = t.matmul(prb[:, p2 * 512:(p2 + 1) * 512],
                                     SEL4[:, hh * 128:hh * 128 + 128],
                                     GR4[:, p2 * 512:(p2 + 1) * 512],
                                     start=True, stop=True)
                        ins.then_inc(sP.h, 1)
                sP.inc(2 * HG)
                k.pe(g_pe)

                def g_ac(sc, g=g, p0=p0, a0=a0, v0=v0):
                    for hh in range(HG):
                        sc.wait_ge(sP.h, p0 + 2 * hh + 1)
                        if hh > 0:
                            sc.wait_ge(sV.h, v0 + (hh - 1) * 3 + 1)
                        elif g > 0:
                            sc.wait_ge(sV.h, v0 - 2)
                        ps = pga if hh % 2 == 0 else pgb
                        sc.activation(SILt, ps[:, :], AF.Silu).then_inc(sA.h, 1)
                sA.inc(HG)
                k.ac(g_ac)

                def g_dv(v, g=g, p0=p0, v0=v0, a0=a0, yht=yht):
                    for hh in range(HG):
                        hgl = g * HG + hh
                        ps = pga if hh % 2 == 0 else pgb
                        v.wait_ge(sA.h, a0 + hh + 1)
                        v.wait_ge(sSL[8 + hh % 2].h, yht[hh])
                        v.tensor_tensor(OYt, YHp[hh % 2], SILt, ALU.mult).then_inc(sV.h, 1)
                        v.wait_ge(sP.h, p0 + 2 * hh + 2)
                        v.scalar_tensor_tensor(R1[:, hgl * 1024:(hgl + 1) * 1024],
                                               OYt, GNWC[:, :], prb[:, :],
                                               ALU.mult, ALU.mult)
                        v.drain().then_inc(sV.h, 1)
                        v.drain().then_inc(sV.h, 1)
                sV.inc(3 * HG)
                k.dv(g_dv)
            k.emit(bar=False)

        dbg_tap("yg", R1[:, 0:2048])

        with (
            nc.psum_tensor("poa", [128, 1024], F32) as poa,
            nc.psum_tensor("pob", [128, 1024], F32) as pob,
        ):
            k = Blk(nc)
            p0 = sP.n
            v0 = sV.n
            slabt6 = [sSL[ob % 2].inc(16) for ob in range(FIN)]
            xpt6 = [sSL[8 + ob % 2].inc(16) for ob in range(FIN)]

            def o_gp(gp, p0=p0, slabt6=slabt6):
                for ob in range(FIN):
                    buf = WSA if ob % 2 == 0 else WSB
                    if ob >= 2:
                        gp.wait_ge(sP.h, p0 + ob - 1)
                    else:
                        gp.wait_ge(sP.h, p0)
                    gp.dma_start(out=buf[:, :],
                                 in_=dwo[ob * 128:(ob + 1) * 128, :]
                                 ).then_inc(sSL[ob % 2].h, 16)
            k.gp(o_gp)

            def o_sy(s, v0=v0, xpt6=xpt6):
                for ob in range(FIN):
                    if ob >= 2:
                        s.wait_ge(sV.h, v0 + ob - 1)
                    else:
                        s.wait_ge(sV.h, v0)
                    s.dma_start(out=XPTp[ob % 2],
                                in_=xpark[ob * 128:(ob + 1) * 128, :]
                                ).then_inc(sSL[8 + ob % 2].h, 16)
            k.sy(o_sy)

            def o_pe(t, v0=v0, slabt6=slabt6):
                t.wait_ge(sV.h, v0)
                for ob in range(FIN):
                    t.wait_ge(sSL[ob % 2].h, slabt6[ob])
                    if ob >= 2:
                        t.wait_ge(sV.h, v0 + ob - 1)
                    ps = poa if ob % 2 == 0 else pob
                    buf = WSA if ob % 2 == 0 else WSB
                    for hd in range(FIN):
                        for p2 in range(2):
                            ins = t.matmul(ps[:, p2 * 512:(p2 + 1) * 512],
                                     buf[:, hd * 128:(hd + 1) * 128],
                                     R1[:, hd * 1024 + p2 * 512:hd * 1024 + (p2 + 1) * 512],
                                     start=(hd == 0), stop=(hd == FIN - 1))
                    ins.then_inc(sP.h, 1)
            sP.inc(FIN)
            k.pe(o_pe)

            def o_dv(v, p0=p0, xpt6=xpt6):
                for ob in range(FIN):
                    v.wait_ge(sP.h, p0 + ob + 1)
                    v.wait_ge(sSL[8 + ob % 2].h, xpt6[ob])
                    ps = poa if ob % 2 == 0 else pob
                    v.tensor_tensor(R2[:, ob * 1024:(ob + 1) * 1024], ps[:, :],
                                    XPTp[ob % 2], ALU.add).then_inc(sV.h, 1)
            oDvB = v0
            sV.inc(FIN)
            k.dv(o_dv)
            k.emit(bar=False)

        dbg_tap("xnew", R2[:, 0:2048])

        emit_ln(R2, None, "l3", fold=True, srcw=oDvB)
        emit_ffn(R2, R2, "f2", final=True, lncorr=True)

        k = Blk(nc)
        xF = sX.n
        s8, s9 = sSL[8].n, sSL[9].n
        k.sy(lambda s: (s.wait_ge(sX.h, xF), s.wait_ge(sSL[8].h, s8),
                        s.wait_ge(sSL[9].h, s9)))
        k.gp(lambda g: g.wait_ge(sX.h, xF))
        k.emit()
        Blk.flush(nc)

    return nc, dbg_names


def _rope_tables(positions):
    inv = 1.0 / 10000.0 ** (np.arange(0, D, 2, dtype=np.float32) / D)
    ang = positions[:, None].astype(np.float32) * inv
    cos = np.concatenate([np.cos(ang), np.cos(ang)], axis=-1)
    sin = np.concatenate([np.sin(ang), np.sin(ang)], axis=-1)
    return cos, sin


def kernel(L0_attn_hidden_states, L0_attn_Wq, L0_attn_Wk, L0_attn_Wv, L0_attn_Wo,
           L0_attn_Wg, L0_attn_Win, L0_attn_dt_bias, L0_attn_A_log_bias,
           L0_attn_gnorm_w, L0_ffn_route_w, L0_ffn_A, L0_ffn_Bw, _debug_taps=()):
    f32 = np.float32
    X = np.ascontiguousarray(L0_attn_hidden_states.reshape(8 * T, HID))

    def blockT(W):
        return np.ascontiguousarray(
            W.reshape(32, 128, 32, 128).transpose(0, 3, 2, 1).reshape(32 * 128, 4096))

    wq = blockT(L0_attn_Wq)
    wg = blockT(L0_attn_Wg)
    wo = blockT(L0_attn_Wo)
    wvk0 = np.concatenate([L0_attn_Wv.T, L0_attn_Wk.T], axis=1)
    wvk = np.ascontiguousarray(
        wvk0.reshape(32, 128, 4, 512).transpose(2, 0, 1, 3).reshape(4 * 32 * 128, 512))
    win = np.ascontiguousarray(
        L0_attn_Win.reshape(32, 32, 128).transpose(2, 1, 0).reshape(128, 1024))
    aall = np.ascontiguousarray(
        L0_ffn_A.reshape(128, 32, 128).transpose(2, 1, 0).reshape(128, 4096))
    route = np.ascontiguousarray(
        L0_ffn_route_w.reshape(8, 32, 128).transpose(2, 1, 0).reshape(128, 256))
    bw = np.ascontiguousarray(L0_ffn_Bw.transpose(0, 2, 1).reshape(128, HID))

    psw = np.zeros((128, 128), f32)
    for m in range(128):
        psw[(m + 64) % 128, m] = 1.0
    idf = np.eye(128, dtype=f32)
    s127 = np.zeros((128, 128), f32); s127[127, :] = 1.0
    sel4 = np.zeros((4, 512), f32)
    for hh in range(4):
        sel4[hh, hh * 128:(hh + 1) * 128] = 1.0
    eye = np.zeros((128, 1024), f32)
    for h in range(32):
        eye[:, h * 32 + h] = 1.0
    mskn = np.where(np.arange(128)[:, None] > np.arange(128)[None, :], -1e30, 0.0
                    ).astype(f32)
    bf = lambda a: np.ascontiguousarray(a.astype(bfloat16))

    in_maps = []
    for c in range(8):
        xc = np.ascontiguousarray(X[c * T:(c + 1) * T, :].T)
        pos = (c % 2) * T + np.arange(T)
        cos, sin = _rope_tables(pos)
        cosf = bf(cos.T)
        sinm_f = sin.T.copy(); sinm_f[0:64, :] *= -1.0
        sinf = bf(sinm_f)
        cost = np.zeros((128, NC8 * D), f32)
        sint = np.zeros((128, NC8 * D), f32)
        for cc in range(NC8):
            cost[:, cc * D:(cc + 1) * D] = cos[cc * 128:(cc + 1) * 128, :]
            sm = sin[cc * 128:(cc + 1) * 128, :].copy()
            sm[:, 0:64] *= -1.0
            sint[:, cc * D:(cc + 1) * D] = sm
        parity = 1.0 if c % 2 == 0 else 0.0
        in_maps.append({
            "dx": bf(xc), "dwq": bf(wq), "dwg": bf(wg), "dwo": bf(wo),
            "dwvk": bf(wvk), "dwin": bf(win),
            "daall": bf(aall), "droute": bf(route), "dbw": bf(bw),
            "dcosf": cosf, "dsinf": sinf, "dcost": bf(cost), "dsint": bf(sint),
            "dpsw": bf(psw), "didf": idf, "didb": bf(idf), "ds127": s127,
            "dsel4": sel4, "deye": bf(eye), "dmskn": bf(mskn),
            "ddtb": np.ascontiguousarray(L0_attn_dt_bias.reshape(32, 1)),
            "dalog": np.ascontiguousarray(L0_attn_A_log_bias.reshape(32, 1)),
            "dgnw": np.ascontiguousarray(L0_attn_gnorm_w.reshape(128, 1)),
            "dsendm": np.full((128, 1), parity, f32),
            "drecvm": np.full((128, 1), 1.0 - parity, f32),
        })

    nc, dbg_names = build(debug_taps=_debug_taps)
    res = run_bass_kernel_spmd(nc, in_maps, core_ids=list(range(8)))

    out = np.empty((8 * T, HID), f32)
    for c in range(8):
        out[c * T:(c + 1) * T, :] = res.results[c]["dout"].T
    if _debug_taps:
        kernel.debug = [{n: r.get("dbg_" + n) for n in dbg_names}
                        for r in res.results]
    return out.reshape(4, 2048, HID)


# version 20
# speedup vs baseline: 1.2017x; 1.2017x over previous
import numpy as np
from contextlib import ExitStack
from ml_dtypes import bfloat16
import concourse.bass as bass
import concourse.mybir as mybir
from concourse.bass_utils import run_bass_kernel_spmd

F32 = mybir.dt.float32
BF16 = mybir.dt.bfloat16
AF = mybir.ActivationFunctionType
ALU = mybir.AluOpType

T = 1024
FIN = 32
HID = 4096
NG = 8
HG = 4
D = 128
NC8 = 8
CH = 128
EPS = 1e-5
RMS_EPS = 1e-5

ENGS = ("tensor", "vector", "scalar", "gpsimd", "sync")

_DATAPATH_OPS = {
    "tensor_tensor", "tensor_add", "tensor_sub", "tensor_mul", "tensor_max",
    "tensor_scalar", "tensor_single_scalar", "tensor_scalar_add",
    "tensor_scalar_sub", "tensor_scalar_mul", "tensor_scalar_max",
    "tensor_scalar_min", "tensor_relu", "tensor_reduce", "reduce_max",
    "reduce_sum", "scalar_tensor_tensor", "tensor_tensor_scan", "tensor_copy",
    "memset", "memzero", "reciprocal", "select", "transpose", "bn_stats",
    "bn_aggr", "activation", "mul", "add", "copy", "square", "sqrt", "sign",
    "tensor_tensor_reduce", "copy_predicated", "stream_shuffle",
}


class DrainProxy:
    def __init__(self, eng):
        object.__setattr__(self, "_e", eng)

    def __getattr__(self, name):
        a = getattr(self._e, name)
        if name in _DATAPATH_OPS and callable(a):
            def wrapped(*args, _a=a, **kw):
                self._e.drain()
                return _a(*args, **kw)
            return wrapped
        return a


class Sem:
    def __init__(self, h):
        self.h = h
        self.n = 0

    def inc(self, a=1):
        self.n += a
        return self.n


class Blk:
    BAR = None
    STREAM = None

    def __init__(self, nc):
        self.nc = nc
        self.q = {e: [] for e in ENGS}

    def pe(self, fn): self.q["tensor"].append(fn)
    def dv(self, fn): self.q["vector"].append(fn)
    def ac(self, fn): self.q["scalar"].append(fn)
    def gp(self, fn): self.q["gpsimd"].append(fn)
    def sy(self, fn): self.q["sync"].append(fn)

    def emit(self, bar=True):
        if bar:
            bsem, cnt = Blk.BAR
            cnt[0] += 1
            tgt = 5 * cnt[0]
            def barf(eng, tgt=tgt):
                eng.drain().then_inc(bsem, 1)
                eng.wait_ge(bsem, tgt)
            tail = [barf]
        else:
            tail = []
        for ename in ENGS:
            Blk.STREAM[ename] += self.q[ename] + tail

    @classmethod
    def flush(cls, nc):
        with nc.Block() as b:
            for ename in ENGS:
                fns = cls.STREAM[ename]
                if not fns:
                    continue
                def body(eng, fns=fns, ename=ename):
                    if ename in ("vector", "scalar"):
                        eng = DrainProxy(eng)
                    for f in fns:
                        f(eng)
                getattr(b, ename)(body)
        cls.STREAM = {e: [] for e in ENGS}


def build(debug_taps=()):
    nc = bass.Bass("TRN2", target_bir_lowering=False, debug=False)
    taps = set(debug_taps)
    dbg_names = []

    dx = nc.dram_tensor("dx", [HID, T], BF16, kind="ExternalInput").ap()
    dwq = nc.dram_tensor("dwq", [32 * 128, 4096], BF16, kind="ExternalInput").ap()
    dwg = nc.dram_tensor("dwg", [32 * 128, 4096], BF16, kind="ExternalInput").ap()
    dwo = nc.dram_tensor("dwo", [32 * 128, 4096], BF16, kind="ExternalInput").ap()
    dwvk = nc.dram_tensor("dwvk", [4 * 32 * 128, 512], BF16, kind="ExternalInput").ap()
    dwin = nc.dram_tensor("dwin", [128, 1024], BF16, kind="ExternalInput").ap()
    daall = nc.dram_tensor("daall", [128, 4096], BF16, kind="ExternalInput").ap()
    droute = nc.dram_tensor("droute", [128, 256], BF16, kind="ExternalInput").ap()
    dbw = nc.dram_tensor("dbw", [128, HID], BF16, kind="ExternalInput").ap()
    dcosf = nc.dram_tensor("dcosf", [128, T], BF16, kind="ExternalInput").ap()
    dsinf = nc.dram_tensor("dsinf", [128, T], BF16, kind="ExternalInput").ap()
    dcost = nc.dram_tensor("dcost", [128, NC8 * D], BF16, kind="ExternalInput").ap()
    dsint = nc.dram_tensor("dsint", [128, NC8 * D], BF16, kind="ExternalInput").ap()
    dpsw = nc.dram_tensor("dpsw", [128, 128], BF16, kind="ExternalInput").ap()
    didf = nc.dram_tensor("didf", [128, 128], F32, kind="ExternalInput").ap()
    didb = nc.dram_tensor("didb", [128, 128], BF16, kind="ExternalInput").ap()
    ds127 = nc.dram_tensor("ds127", [128, 128], F32, kind="ExternalInput").ap()
    dsel4 = nc.dram_tensor("dsel4", [4, 512], F32, kind="ExternalInput").ap()
    deye = nc.dram_tensor("deye", [128, 1024], BF16, kind="ExternalInput").ap()
    dmskn = nc.dram_tensor("dmskn", [128, 128], BF16, kind="ExternalInput").ap()
    ddtb = nc.dram_tensor("ddtb", [32, 1], F32, kind="ExternalInput").ap()
    dalog = nc.dram_tensor("dalog", [32, 1], F32, kind="ExternalInput").ap()
    dgnw = nc.dram_tensor("dgnw", [128, 1], F32, kind="ExternalInput").ap()
    dsendm = nc.dram_tensor("dsendm", [128, 1], F32, kind="ExternalInput").ap()
    drecvm = nc.dram_tensor("drecvm", [128, 1], F32, kind="ExternalInput").ap()
    dout = nc.dram_tensor("dout", [HID, T], F32, kind="ExternalOutput").ap()

    xpark = nc.dram_tensor("xpark", [HID, T], BF16).ap()
    ypark = nc.dram_tensor("ypark", [NG, 128, 4096], BF16).ap()
    ccin = nc.dram_tensor("ccin", [128, 512], F32)
    ccout = nc.dram_tensor("ccout", [128, 512], F32)

    with ExitStack() as ST:
        E = ST.enter_context
        R1 = E(nc.sbuf_tensor("R1", [128, 32768], BF16))
        R2 = E(nc.sbuf_tensor("R2", [128, 32768], BF16))
        R3 = E(nc.sbuf_tensor("R3", [128, 8192], BF16))
        KTB = E(nc.sbuf_tensor("KTB", [128, 8192], BF16))
        WSA = E(nc.sbuf_tensor("WSA", [128, 4096], BF16))
        WSB = E(nc.sbuf_tensor("WSB", [128, 4096], BF16))
        P4T = E(nc.sbuf_tensor("P4T", [128, 3072], BF16))
        SCR = E(nc.sbuf_tensor("SCR", [128, 5120], F32))
        SEL4 = E(nc.sbuf_tensor("SEL4", [4, 512], F32))
        IDF = E(nc.sbuf_tensor("IDF", [128, 128], F32))
        IDB = E(nc.sbuf_tensor("IDB", [128, 128], BF16))
        S127 = E(nc.sbuf_tensor("S127", [128, 128], F32))
        PSW = E(nc.sbuf_tensor("PSW", [128, 128], BF16))
        O1F = E(nc.sbuf_tensor("O1F", [1, 128], F32))
        OCB = E(nc.sbuf_tensor("OCB", [128, 1], BF16))
        ZR32 = E(nc.sbuf_tensor("ZR32", [32, 128], F32))
        DTBC = E(nc.sbuf_tensor("DTBC", [32, 1], F32))
        ALGC = E(nc.sbuf_tensor("ALGC", [32, 1], F32))
        ACOL = E(nc.sbuf_tensor("ACOL", [32, 1], F32))
        GNWC = E(nc.sbuf_tensor("GNWC", [128, 1], F32))
        SENDM = E(nc.sbuf_tensor("SENDM", [128, 1], F32))
        RECVM = E(nc.sbuf_tensor("RECVM", [128, 1], F32))

        QG = R1[:, 0:4096]
        QS = R1[:, 4096:8192]
        YGRB = R1[:, 8192:12288]
        YGR = R1[:, 12288:16384]
        CST = R1[:, 16384:20480]
        A4SB = R1[:, 20480:21504].bitcast(F32)
        DEC4B = R1[:, 21504:22528].bitcast(F32)
        PRV = R1[:, 24576:25600].bitcast(F32)
        CARRY = R1[:, 25600:26624].bitcast(F32)
        CRB = R1[:, 26624:27648].bitcast(F32)
        COSB = R1[:, 27648:28672]
        SINM = R1[:, 28672:29696]
        EYE = R1[:, 29696:30720]
        MSKN = R1[:, 30720:30848]
        CTT = R1[:, 30848:31872]
        STT = R1[:, 16384:17408]
        PRVB2 = R1[:, 30848:31360]
        A4SC = R1[:, 22528:23552].bitcast(F32)
        PRVB2B = R1[:, 20480:20992]
        A4SD = R1[:, 23552:24576].bitcast(F32)
        PRV2 = R1[:, 21504:22528].bitcast(F32)
        SQY2 = R1[:, 31360:32384]

        DTF = SCR[0:32, 0:1024]
        GFT = SCR[0:32, 1024:2048]
        T1 = SCR[0:32, 2048:3072]
        GR4 = SCR[0:4, 2048:3072]
        NGT = SCR[:, 3072:3328]
        DTTc = SCR[:, 3328:3584]
        EDT = SCR[:, 3584:3840]
        CDB = SCR[:, 3840:4096]
        A4S = SCR[:, 4096:4608]
        E1B = A4S
        DEC4 = SCR[:, 4608:5120]
        ROWA = SCR[0:1, 0:1024]
        ROWB = SCR[0:1, 1024:2048]
        LG8 = SCR[0:8, 2048:3072]
        MXC = SCR[0:128, 5119:5120]
        SSQ = SCR[0:32, 0:1024]
        RSTD = SCR[0:32, 1024:2048]
        CMS = SCR[:, 4096:4608]

        QRW = P4T[:, 0:1024]
        SM4A = P4T[:, 0:512]
        SM4B = P4T[:, 512:1024]
        PRVB = P4T[:, 0:512]
        SQY = P4T[:, 0:1024]
        O2S = P4T[:, 1024:2048]
        KD4A = P4T[:, 1024:1536]
        KD4B = P4T[:, 1536:2048]
        KFG = P4T[:, 2048:3072]

        VTB = R3
        SQTa = R3[:, 0:1024]
        SQTb = R3[:, 1024:2048]
        WHF = R3[:, 2048:3072]
        WHT = R3[:, 3072:3200]
        C0S = R3[:, 3328:4352]
        C1S = R3[:, 4352:5376]
        OTMPa = R3[:, 5376:7424].bitcast(F32)
        OTMPb = R3[:, 0:2048].bitcast(F32)
        KR1 = P4T[:, 0:1024]
        KR2 = P4T[:, 1024:2048]
        YHp = (R3[:, 0:1024], R3[:, 1024:2048])
        SILt = R3[:, 2048:3072]
        OYt = R3[:, 3072:4096]
        XPTp = (R3[:, 0:1024], R3[:, 1024:2048])

        sW = Sem(E(nc.semaphore("sW")))
        sX = Sem(E(nc.semaphore("sX")))
        sP = Sem(E(nc.semaphore("sP")))
        sV = Sem(E(nc.semaphore("sV")))
        sA = Sem(E(nc.semaphore("sA")))
        sC = Sem(E(nc.semaphore("sC")))
        sB = E(nc.semaphore("sB"))
        Blk.BAR = (sB, [0])
        Blk.STREAM = {e: [] for e in ENGS}
        sSL = [Sem(E(nc.semaphore(f"sS{i}"))) for i in range(12)]
        sQW = [Sem(E(nc.semaphore(f"sQW{i}"))) for i in range(2)]
        sG = Sem(E(nc.semaphore("sG")))
        sPa = Sem(E(nc.semaphore("sPa")))

        def dbg_tap(name, ap):
            if name not in taps:
                return
            o = nc.dram_tensor("dbg_" + name, list(ap.shape), ap.dtype,
                               kind="ExternalOutput").ap()
            dbg_names.append(name)
            k = Blk(nc)
            tgt = sX.inc(16)
            k.sy(lambda s, ap=ap, o=o, tgt=tgt: (
                s.dma_start(out=o, in_=ap).then_inc(sX.h, 16),
                s.wait_ge(sX.h, tgt)))
            k.emit()

        k = Blk(nc)
        def b0_gp(g):
            for i in range(4):
                src = dx[i * 1024:(i + 1) * 1024, :].rearrange("(f p) t -> p f t", p=128)
                dst = R1[:, i * 8192:(i + 1) * 8192].rearrange("p (f t) -> p f t", f=8)
                g.dma_start(out=dst, in_=src).then_inc(sW.h, 16)
        sW.inc(64)
        k.gp(b0_gp)
        def b0_sy(s):
            for dst, src in ((PSW[:, :], dpsw), (IDF[:, :], didf), (IDB[:, :], didb),
                             (S127[:, :], ds127), (SEL4[:, :], dsel4), (DTBC[:, :], ddtb),
                             (ALGC[:, :], dalog), (GNWC[:, :], dgnw),
                             (SENDM[:, :], dsendm), (RECVM[:, :], drecvm)):
                s.dma_start(out=dst, in_=src).then_inc(sX.h, 16)
        sX.inc(160)
        k.sy(b0_sy)
        wT0, xT0 = sW.n, sX.n
        def b0_dv(v):
            v.memset(O1F[:, :], 1.0)
            v.memset(OCB[:, :], 1.0)
            v.memset(ZR32[:, :], 0.0)
            v.wait_ge(sX.h, xT0)
            v.wait_ge(sW.h, wT0)
        k.dv(b0_dv)
        def b0_ac(sc):
            sc.wait_ge(sX.h, xT0)
            sc.activation(ACOL[:, :], ALGC[:, :], AF.Exp).then_inc(sA.h, 1)
        sA.inc()
        k.ac(b0_ac)
        k.pe(lambda t: (t.wait_ge(sW.h, wT0), t.wait_ge(sX.h, xT0)))
        k.emit()

        k = Blk(nc)
        aT0 = sA.n
        k.dv(lambda v: (v.wait_ge(sA.h, aT0),
                        v.tensor_scalar_mul(ACOL[:, :], ACOL[:, :], -1.0)))
        k.emit()

        def emit_ln(src, dst, tag, fold=False, srcw=None):
            with (
                nc.psum_tensor("pl1" + tag, [1, 1024], F32) as ps1,
                nc.psum_tensor("pl2" + tag, [1, 1024], F32) as ps2,
                nc.psum_tensor("pb0" + tag, [128, 1024], F32) as pb0,
                nc.psum_tensor("pb1" + tag, [128, 1024], F32) as pb1,
            ):
                k = Blk(nc)
                v0, p0, aq0 = sV.n, sP.n, sA.n
                def ln_dv(v):
                    if srcw is not None:
                        v.wait_ge(sV.h, srcw + FIN)
                    for f in range(0, FIN, 2):
                        if f >= 2:
                            v.wait_ge(sP.h, p0 + f - 1)
                        v.tensor_mul(SQTa, src[:, f * 1024:(f + 1) * 1024],
                                     src[:, f * 1024:(f + 1) * 1024]).then_inc(sV.h, 1)
                sV.inc(FIN // 2)
                k.dv(ln_dv)
                def ln_sq_ac(sc):
                    if srcw is not None:
                        sc.wait_ge(sV.h, srcw + FIN)
                    for f in range(1, FIN, 2):
                        if f >= 2:
                            sc.wait_ge(sP.h, p0 + f - 1)
                        sc.square(SQTb, src[:, f * 1024:(f + 1) * 1024]
                                  ).then_inc(sA.h, 1)
                sA.inc(FIN // 2)
                k.ac(ln_sq_ac)
                def ln_pe(t):
                    if srcw is not None:
                        t.wait_ge(sV.h, srcw + FIN)
                    for f in range(FIN):
                        for hh in range(2):
                            t.matmul(ps1[:, hh * 512:(hh + 1) * 512], OCB[:, :],
                                     src[:, f * 1024 + hh * 512:f * 1024 + (hh + 1) * 512],
                                     start=(f == 0), stop=(f == FIN - 1))
                    for f in range(FIN):
                        buf = SQTa if f % 2 == 0 else SQTb
                        if f % 2 == 0:
                            t.wait_ge(sV.h, v0 + f // 2 + 1)
                        else:
                            t.wait_ge(sA.h, aq0 + f // 2 + 1)
                        t.matmul(ps2[:, 0:512], OCB[:, :], buf[:, 0:512],
                                 start=(f == 0), stop=(f == FIN - 1))
                        t.matmul(ps2[:, 512:1024], OCB[:, :], buf[:, 512:1024],
                                 start=(f == 0), stop=(f == FIN - 1)).then_inc(sP.h, 1)
                sP.inc(FIN)
                k.pe(ln_pe)
                pT = sP.n
                def st_dv(v):
                    v.wait_ge(sP.h, pT)
                    v.tensor_scalar_mul(ROWA, ps1[:, :], 1.0 / HID)
                    v.tensor_mul(ROWB, ROWA, ROWA)
                    v.scalar_tensor_tensor(ROWB, ps2[:, :], 1.0 / HID, ROWB,
                                           ALU.mult, ALU.subtract)
                    v.tensor_scalar_add(ROWB, ROWB, EPS).then_inc(sV.h, 1)
                sV.inc()
                k.dv(st_dv)
                vT = sV.n
                def st_ac(sc):
                    sc.wait_ge(sV.h, vT)
                    sc.activation(ROWB, ROWB, AF.Ln)
                    sc.activation(ROWB, ROWB, AF.Exp, scale=-0.5).then_inc(sA.h, 1)
                sA.inc()
                k.ac(st_ac)
                aT = sA.n
                def st_dv2(v):
                    v.wait_ge(sA.h, aT)
                    v.scalar_tensor_tensor(ROWA, ROWA, -1.0, ROWB,
                                           ALU.mult, ALU.mult).then_inc(sV.h, 1)
                sV.inc()
                k.dv(st_dv2)
                vT2 = sV.n
                def st_pe(t):
                    t.wait_ge(sV.h, vT2)
                    for hh in range(2):
                        t.matmul(pb0[:, hh * 512:(hh + 1) * 512], O1F[:, :],
                                 ROWA[:, hh * 512:(hh + 1) * 512], start=True, stop=True)
                        ins = t.matmul(pb1[:, hh * 512:(hh + 1) * 512], O1F[:, :],
                                 ROWB[:, hh * 512:(hh + 1) * 512], start=True, stop=True)
                    ins.then_inc(sP.h, 1)
                sP.inc()
                k.pe(st_pe)
                pT2 = sP.n
                appB = sV.n + 1
                def app_dv(v, fold=fold):
                    v.wait_ge(sP.h, pT2)
                    v.tensor_copy(C0S, pb0[:, :])
                    v.tensor_copy(C1S, pb1[:, :]).then_inc(sV.h, 1)
                    if not fold:
                        for f in range(FIN):
                            v.tensor_tensor(dst[:, f * 1024:(f + 1) * 1024],
                                            src[:, f * 1024:(f + 1) * 1024], C1S, ALU.mult)
                            v.tensor_tensor(dst[:, f * 1024:(f + 1) * 1024],
                                            dst[:, f * 1024:(f + 1) * 1024], C0S,
                                            ALU.add).then_inc(sV.h, 1)
                sV.inc(1 if fold else 1 + FIN)
                k.dv(app_dv)
                k.emit(bar=fold)
                return appB

        def emit_ffn(xln, resid, tag, final=False, lncorr=False):
            with (
                nc.psum_tensor("pfh" + tag, [128, 1024], F32) as pha,
                nc.psum_tensor("pfl" + tag, [8, 1024], F32) as plg,
                nc.psum_tensor("pff" + tag, [128, 8], F32) as ptf,
                nc.psum_tensor("pfb" + tag, [128, 128], BF16) as ptb,
                nc.psum_tensor("pfo" + tag, [128, 1024], F32) as pbo,
            ):
                k = Blk(nc)
                w0 = sW.n
                def fw_gp(g):
                    g.dma_start(out=WSA[:, :], in_=daall).then_inc(sW.h, 16)
                    g.dma_start(out=WSB[:, 0:256], in_=droute).then_inc(sW.h, 16)
                sW.inc(32)
                k.gp(fw_gp)
                wT = sW.n
                def fh_pe(t, lncorr=lncorr):
                    t.wait_ge(sW.h, wT)
                    if lncorr:
                        for f in range(FIN):
                            t.matmul(ptf[:, 0:1], WSA[:, f * 128:(f + 1) * 128],
                                     OCB[:, :], start=(f == 0), stop=(f == FIN - 1))
                        for f in range(FIN):
                            t.matmul(ptf[0:8, 1:2], WSB[:, f * 8:(f + 1) * 8],
                                     OCB[:, :], start=(f == 0), stop=(f == FIN - 1))
                    for f in range(FIN):
                        for hh in range(2):
                            t.matmul(pha[:, hh * 512:(hh + 1) * 512],
                                     WSA[:, f * 128:(f + 1) * 128],
                                     xln[:, f * 1024 + hh * 512:f * 1024 + (hh + 1) * 512],
                                     start=(f == 0), stop=(f == FIN - 1))
                    for f in range(FIN):
                        for hh in range(2):
                            ins = t.matmul(plg[:, hh * 512:(hh + 1) * 512],
                                     WSB[:, f * 8:(f + 1) * 8],
                                     xln[:, f * 1024 + hh * 512:f * 1024 + (hh + 1) * 512],
                                     start=(f == 0), stop=(f == FIN - 1))
                    ins.then_inc(sP.h, 1)
                sP.inc()
                k.pe(fh_pe)
                pT = sP.n
                CSA = SCR[0:128, 5118:5119]
                CSR = SCR[0:8, 5117:5118]
                LTMP = SCR[0:8, 4096:5117]
                def fe_dv(v, lncorr=lncorr):
                    v.wait_ge(sP.h, pT)
                    if lncorr:
                        v.tensor_copy(CSA, ptf[:, 0:1])
                        v.tensor_copy(CSR, ptf[0:8, 1:2])
                        v.tensor_tensor(SQTb, pha[:, :], C1S, ALU.mult)
                        v.scalar_tensor_tensor(SQTa, C0S, CSA, SQTb, ALU.mult, ALU.add)
                        v.tensor_tensor(LTMP[:, 0:1021] if False else LG8, plg[:, :],
                                        C1S[0:8, :], ALU.mult)
                        ins = v.scalar_tensor_tensor(LG8, C0S[0:8, :].bitcast(F32)
                                                     if False else C0S[0:8, :],
                                                     CSR, LG8, ALU.mult, ALU.add)
                        ins.then_inc(sV.h, 1)
                    else:
                        v.tensor_copy(SQTa, pha[:, :])
                        v.tensor_copy(LG8, plg[:, :]).then_inc(sV.h, 1)
                sV.inc()
                k.dv(fe_dv)
                v0 = sV.n
                p1 = sP.n
                def fr_pe(t):
                    t.wait_ge(sV.h, v0)
                    for c in range(NC8):
                        if c > 0:
                            t.wait_ge(sV.h, v0 + c)
                        t.transpose(ptf[:, :], LG8[:, c * 128:(c + 1) * 128],
                                    IDF[0:8, 0:8]).then_inc(sP.h, 1)
                sP.inc(NC8)
                k.pe(fr_pe)
                def fr_dv(v):
                    for c in range(NC8):
                        v.wait_ge(sP.h, p1 + c + 1)
                        v.tensor_reduce(MXC, ptf[:, :], mybir.AxisListType.X, ALU.max)
                        v.tensor_scalar(DEC4[:, c * 8:(c + 1) * 8], ptf[:, :],
                                        MXC, MXC, ALU.is_equal, ALU.mult).then_inc(sV.h, 1)
                sV.inc(NC8)
                k.dv(fr_dv)

                v1 = sV.n
                p2 = sP.n
                def fw_pe(t, v0=v0):
                    t.wait_ge(sV.h, v0)
                    for c in range(NC8):
                        if c > 0:
                            t.wait_ge(sV.h, v1 + 2 * c)
                        t.transpose(ptb[:, :], SQTa[:, c * 128:(c + 1) * 128],
                                    IDB[:, :]).then_inc(sP.h, 1)
                        t.wait_ge(sV.h, v1 + 2 * c + 1)
                        t.transpose(ptb[:, :], WHT, IDB[:, :]).then_inc(sP.h, 1)
                sP.inc(2 * NC8)
                k.pe(fw_pe)
                def fw_dv(v):
                    for c in range(NC8):
                        v.wait_ge(sP.h, p2 + 2 * c + 1)
                        wsel3 = DEC4[:, c * 8:(c + 1) * 8].unsqueeze(2).broadcast_to([128, 8, 16])
                        v.tensor_tensor(WHT.rearrange("p (e r) -> p e r", e=8),
                                        ptb[:, :].rearrange("p (e r) -> p e r", e=8),
                                        wsel3, ALU.mult).then_inc(sV.h, 1)
                        v.wait_ge(sP.h, p2 + 2 * c + 2)
                        v.tensor_copy(WHF[:, c * 128:(c + 1) * 128], ptb[:, :]).then_inc(sV.h, 1)
                sV.inc(2 * NC8)
                k.dv(fw_dv)

                def fb_gp(g, pT=pT):
                    g.wait_ge(sP.h, pT)
                    g.dma_start(out=WSA[:, :], in_=dbw).then_inc(sW.h, 16)
                sW.inc(16)
                k.gp(fb_gp)
                wT2 = sW.n
                def fb_dv0(v):
                    v.wait_ge(sW.h, wT2)
                    v.tensor_scalar_mul(WSA[:, :], WSA[:, :], 0.5).then_inc(sV.h, 1)
                sV.inc()
                k.dv(fb_dv0)
                v2 = sV.n
                p3 = sP.n
                stslots = [sSL[8 + ot % 2].inc(16) for ot in range(FIN)] if final else None
                def fb_pe(t):
                    t.wait_ge(sV.h, v2)
                    for ot in range(FIN):
                        if ot > 1:
                            t.wait_ge(sV.h, v2 + ot - 1)
                        ps = pbo if ot % 2 == 0 else pha
                        for hh in range(2):
                            ins = t.matmul(ps[:, hh * 512:(hh + 1) * 512],
                                     WSA[:, ot * 128:(ot + 1) * 128],
                                     WHF[:, hh * 512:(hh + 1) * 512],
                                     start=True, stop=True)
                        ins.then_inc(sP.h, 1)
                sP.inc(FIN)
                k.pe(fb_pe)
                def fb_dv(v):
                    for ot in range(FIN):
                        v.wait_ge(sP.h, p3 + ot + 1)
                        ps = pbo if ot % 2 == 0 else pha
                        if final:
                            buf = OTMPa if ot % 2 == 0 else OTMPb
                            if ot >= 2:
                                v.wait_ge(sSL[8 + ot % 2].h, stslots[ot - 2])
                            v.tensor_tensor(buf, ps[:, :],
                                            resid[:, ot * 1024:(ot + 1) * 1024], ALU.add)
                            v.drain().then_inc(sV.h, 1)
                        else:
                            v.tensor_tensor(resid[:, ot * 1024:(ot + 1) * 1024], ps[:, :],
                                            resid[:, ot * 1024:(ot + 1) * 1024],
                                            ALU.add).then_inc(sV.h, 1)
                sV.inc(FIN)
                k.dv(fb_dv)
                if final:
                    def fb_sy(s):
                        for ot in range(FIN):
                            buf = OTMPa if ot % 2 == 0 else OTMPb
                            s.wait_ge(sV.h, v2 + 1 + ot)
                            s.dma_start(out=dout[ot * 128:(ot + 1) * 128, :],
                                        in_=buf).then_inc(sSL[8 + ot % 2].h, 16)
                    k.sy(fb_sy)
                k.emit()

        emit_ln(R1, None, "l1", fold=True)
        emit_ffn(R1, R1, "f1", lncorr=True)
        dbg_tap("x1", R1[:, 0:2048])
        app2 = emit_ln(R1, R2, "l2")
        dbg_tap("xln2", R2[:, 0:2048])

        k = Blk(nc)
        parkQ23 = sX.n + 32
        constT = sX.n + 64 + 96
        def park_sy(s, parkQ23=parkQ23):
            for i in (2, 3):
                src = R1[:, i * 8192:(i + 1) * 8192].rearrange("p (f t) -> p f t", f=8)
                dst = xpark[i * 1024:(i + 1) * 1024, :].rearrange("(f p) t -> p f t", p=128)
                s.dma_start(out=dst, in_=src).then_inc(sX.h, 16)
            s.wait_ge(sX.h, parkQ23)
            s.wait_ge(sV.h, app2 + FIN)
            for i in (0, 1):
                src = R1[:, i * 8192:(i + 1) * 8192].rearrange("p (f t) -> p f t", f=8)
                dst = xpark[i * 1024:(i + 1) * 1024, :].rearrange("(f p) t -> p f t", p=128)
                s.dma_start(out=dst, in_=src).then_inc(sX.h, 16)
            for dst, src in ((COSB, dcosf), (SINM, dsinf), (EYE, deye), (MSKN, dmskn),
                             (CTT, dcost), (STT, dsint)):
                s.dma_start(out=dst, in_=src).then_inc(sX.h, 16)
        sX.inc(160)
        k.sy(park_sy)
        k.emit(bar=False)

        with nc.psum_tensor("pz", [32, 1024], F32) as pz:
            k = Blk(nc)
            def z_gp(g):
                g.dma_start(out=WSB[:, 0:1024], in_=dwin).then_inc(sW.h, 16)
            sW.inc(16)
            k.gp(z_gp)
            wT = sW.n
            def z_pe(t, wT=wT, app2=app2):
                t.wait_ge(sW.h, wT)
                for f in range(FIN):
                    t.wait_ge(sV.h, app2 + f + 1)
                    for hh in range(2):
                        ins = t.matmul(pz[:, hh * 512:(hh + 1) * 512], WSB[:, f * 32:(f + 1) * 32],
                                 R2[:, f * 1024 + hh * 512:f * 1024 + (hh + 1) * 512],
                                 start=(f == 0), stop=(f == FIN - 1))
                ins.then_inc(sP.h, 1)
            sP.inc()
            k.pe(z_pe)
            zPeT = pT = sP.n
            def z_ac(sc, pT=pT):
                sc.wait_ge(sP.h, pT)
                sc.activation(DTF, pz[:, :], AF.Relu, bias=DTBC[:, :])
                sc.activation(T1, pz[:, :], AF.Abs, bias=DTBC[:, :])
                sc.activation(GFT, T1, AF.Exp, scale=-1.0)
                sc.activation(T1, GFT, AF.Ln, bias=1.0).then_inc(sA.h, 1)
            sA.inc()
            k.ac(z_ac)
            zAcT = aT = sA.n
            def z_dv(v, aT=aT):
                v.wait_ge(sA.h, aT)
                v.tensor_add(DTF, DTF, T1)
                v.tensor_scalar_mul(GFT, DTF, ACOL[:, :])
                for c in range(NC8):
                    v.tensor_tensor_scan(GFT[:, c * 128:(c + 1) * 128],
                                         GFT[:, c * 128:(c + 1) * 128],
                                         ZR32[:, :], 0.0, ALU.add, ALU.add)
                v.drain().then_inc(sV.h, 1)
            sV.inc()
            k.dv(z_dv)
            vZ = sV.n
            vT = sV.n
            def e2_ac(sc, vT=vT):
                sc.wait_ge(sV.h, vT)
                for c in range(NC8):
                    sc.activation(T1[:, c * 128:(c + 1) * 128],
                                  GFT[:, c * 128:(c + 1) * 128], AF.Exp, scale=-1.0,
                                  bias=GFT[:, c * 128 + 127:c * 128 + 128])
                sc.drain().then_inc(sA.h, 1)
            sA.inc()
            k.ac(e2_ac)
            aT2 = sA.n
            def e2_dv(v, aT2=aT2):
                v.wait_ge(sA.h, aT2)
                v.tensor_mul(T1, T1, DTF).then_inc(sV.h, 1)
            sV.inc()
            k.dv(e2_dv)
            vE = sV.n
            def e2_ac2(sc, vE=vE):
                sc.wait_ge(sV.h, vE)
                sc.activation(DTF, DTF, AF.Ln).then_inc(sA.h, 1)
            sA.inc()
            k.ac(e2_ac2)
            aE = sA.n
            def e2_dv2(v, aE=aE):
                v.wait_ge(sA.h, aE)
                v.tensor_sub(DTF, DTF, GFT).then_inc(sV.h, 1)
            sV.inc()
            k.dv(e2_dv2)
            vT2 = sV.n
            k.emit(bar=False)

        dbg_tap("gft", SCR[0:32, 1024:2048])

        kvEvac = []
        with nc.psum_tensor("pkv", [128, 4096], F32) as pkv:
            k = Blk(nc)
            for q in range(4):
                w0 = sW.n
                p0 = sP.n
                slot_tgts = []
                for f in range(FIN):
                    slot_tgts.append(sSL[f % 8].inc(16))
                def kv_gp(g, q=q, p0=p0, slot_tgts=slot_tgts):
                    for f in range(FIN):
                        if f >= 8:
                            g.wait_ge(sP.h, p0 + f - 7)
                        elif q > 0:
                            g.wait_ge(sP.h, p0)
                        base = (q * 32 + f) * 128
                        g.dma_start(out=WSA[:, (f % 8) * 512:(f % 8 + 1) * 512],
                                    in_=dwvk[base:base + 128, :]).then_inc(sSL[f % 8].h, 16)
                k.gp(kv_gp)
                vPrevEv = kvEvac[-1] if kvEvac else None
                def kv_pe(t, q=q, slot_tgts=slot_tgts, vPrevEv=vPrevEv,
                          zAcT=zAcT):
                    if q == 0:
                        t.wait_ge(sA.h, zAcT)
                    else:
                        t.wait_ge(sV.h, vPrevEv)
                    for f in range(FIN):
                        t.wait_ge(sSL[f % 8].h, slot_tgts[f])
                        if q == 0:
                            t.wait_ge(sV.h, app2 + f + 1)
                        for c in range(NC8):
                            ins = t.matmul(pkv[:, c * 512:(c + 1) * 512],
                                     R2[:, f * 1024 + c * 128:f * 1024 + (c + 1) * 128],
                                     WSA[:, (f % 8) * 512:(f % 8 + 1) * 512],
                                     start=(f == 0), stop=(f == FIN - 1))
                        ins.then_inc(sP.h, 1)
                sP.inc(FIN)
                k.pe(kv_pe)
                pT = sP.n
                def kv_dv(v, q=q, pT=pT):
                    v.wait_ge(sP.h, pT)
                    for c in range(NC8):
                        if q < 2:
                            v.tensor_copy(VTB[:, c * 1024 + q * 512:c * 1024 + (q + 1) * 512],
                                          pkv[:, c * 512:(c + 1) * 512])
                        else:
                            v.tensor_copy(
                                KTB[:, c * 1024 + (q - 2) * 512:c * 1024 + (q - 1) * 512],
                                pkv[:, c * 512:(c + 1) * 512])
                    v.drain().then_inc(sV.h, 1)
                sV.inc()
                kvEvac.append(sV.n)
                k.dv(kv_dv)
            k.emit(bar=False)
        vEvac3 = kvEvac[-1]

        with (
            nc.psum_tensor("ptr", [128, 32], F32) as ptr,
            nc.psum_tensor("pcd", [128, 32], F32) as pcd,
        ):
            k = Blk(nc)
            p3b = sP.n
            a3b = sA.n
            trDvB = sV.n
            def tr_pe(t, trDvB=trDvB, vT2=vT2, vEvac3=vEvac3):
                t.wait_ge(sV.h, vT2)
                t.wait_ge(sV.h, vEvac3)
                for c in range(NC8):
                    t.transpose(ptr[:, :], GFT[:, c * 128:(c + 1) * 128],
                                IDF[0:32, 0:32]).then_inc(sP.h, 1)
                    t.wait_ge(sV.h, trDvB + 3 * c + 1)
                    t.transpose(ptr[:, :], DTF[:, c * 128:(c + 1) * 128],
                                IDF[0:32, 0:32]).then_inc(sP.h, 1)
                    t.wait_ge(sV.h, trDvB + 3 * c + 2)
                    t.transpose(ptr[:, :], T1[:, c * 128:(c + 1) * 128],
                                IDF[0:32, 0:32]).then_inc(sP.h, 1)
                    t.wait_ge(sV.h, trDvB + 3 * c + 3)
                    if c > 0:
                        t.wait_ge(sA.h, a3b + c)
                    t.matmul(pcd[:, :], S127[:, :], NGT[:, c * 32:(c + 1) * 32],
                             start=True, stop=True).then_inc(sP.h, 1)
            sP.inc(4 * NC8)
            k.pe(tr_pe)
            def tr_dv(v, p3b=p3b):
                for c in range(NC8):
                    v.wait_ge(sP.h, p3b + 4 * c + 1)
                    v.tensor_scalar_mul(NGT[:, c * 32:(c + 1) * 32], ptr[:, :], -1.0)
                    v.drain().then_inc(sV.h, 1)
                    v.wait_ge(sP.h, p3b + 4 * c + 2)
                    v.tensor_copy(DTTc[:, c * 32:(c + 1) * 32], ptr[:, :])
                    v.drain().then_inc(sV.h, 1)
                    v.wait_ge(sP.h, p3b + 4 * c + 3)
                    v.tensor_copy(EDT[:, c * 32:(c + 1) * 32], ptr[:, :])
                    v.drain().then_inc(sV.h, 1)
            sV.inc(3 * NC8)
            k.dv(tr_dv)
            def tr_ac(sc, p3b=p3b):
                for c in range(NC8):
                    sc.wait_ge(sP.h, p3b + 4 * c + 4)
                    sc.activation(CDB[:, c * 32:(c + 1) * 32], pcd[:, :], AF.Exp,
                                  scale=-1.0)
                    sc.drain().then_inc(sA.h, 1)
            sA.inc(NC8)
            k.ac(tr_ac)
            k.emit(bar=False)
        vTRtot = sV.n
        aTRtot = sA.n
        trT = sP.n

        with (
            nc.psum_tensor("pqa", [128, 1024], F32) as pqa,
            nc.psum_tensor("prt", [128, 1024], F32) as prt,
            nc.psum_tensor("pst", [128, 1024], F32) as pst,
            nc.psum_tensor("py4", [128, 512], F32) as py4,
            nc.psum_tensor("pc4", [128, 512], F32) as pc4,
        ):
            k = Blk(nc)
            k.dv(lambda v: (v.wait_ge(sX.h, constT), v.wait_ge(sP.h, trT),
                            v.memset(SSQ, 0.0)))
            k.emit(bar=False)

            xCRB = {}
            xPark = {}
            slabT = {}
            prevT = None

            A4P = [A4S, A4SB, A4SC, A4SD]
            def A4Sp(c):
                return A4P[c % 4]

            SLOT = [0, 1, 2, 0, 1, 2, 0, 2]
            prevslot = {}
            last = {}
            for _c in range(NC8):
                prevslot[_c] = last.get(SLOT[_c])
                last[SLOT[_c]] = _c

            def SBK(c):
                s = SLOT[c]
                if s == 0:
                    return pst[:, 0:512], pst[:, 512:1024]
                if s == 1:
                    return pqa[:, 0:512], pqa[:, 512:1024]
                return py4[:, :], pc4[:, :]

            def PRVp(j):
                return PRV if j % 2 == 0 else PRV2

            def SM4p(c):
                return SM4A if c % 2 == 0 else SM4B

            def KD4p(c):
                return KD4A if c % 2 == 0 else KD4B

            def YGRp(gg):
                return YGR if gg % 2 == 0 else YGRB

            k = Blk(nc)
            for g in range(NG + 1):
                has_a = g < NG
                has_p2 = g > 0
                pS, vS, aS, gS = sP.n, sV.n, sA.n, sG.n
                if has_p2:
                    xCRB[g - 1] = sW.inc(16)
                if has_a:
                    xin = sX.inc(16)

                peo = []
                if has_a:
                    peo += [("qp", 0), ("qp", 1)]
                    if has_p2:
                        peo += [("inter", c) for c in range(4)]
                    peo += [("rot", 0), ("qp", 2)]
                    if has_p2:
                        peo += [("inter", c) for c in range(4, NC8)]
                        peo.append(("ssq", 0))
                    peo += [("rot", 1), ("qp", 3)]
                    if has_p2:
                        peo.append(("ssq", 1))
                    peo += [("rot", 2), ("rot", 3)]
                    if has_p2:
                        peo += [("ssq", 2), ("ssq", 3)]
                    peo += [("stgb", 0), ("stgb", 1), ("stgb", 2)]
                    for c in range(NC8):
                        peo += [("ymm", c), ("cmm", c)]
                        if c + 3 < NC8:
                            peo.append(("stgb", c + 3))
                else:
                    peo += [("inter", c) for c in range(NC8)]
                    peo += [("ssq", hh) for hh in range(HG)]
                ptg = {}
                pa_tg = {}
                pi, pai = sP.n, sPa.n
                for op in peo:
                    if op[0] == "stgb":
                        pai += 1
                        pa_tg[op] = pai
                    else:
                        pi += 1
                        ptg[op] = pi

                dveo = []
                if g == 0:
                    dveo += [("krope", c) for c in range(NC8)]
                if has_p2:
                    dveo.append(("prvi", 0))
                    for j in range(NC8):
                        if j == 6 and has_a:
                            dveo.append(("evac", 0))
                        dveo.append(("pvb", j))
                        if j < NC8 - 1:
                            dveo.append(("scan", j))
                        if 2 <= j:
                            dveo.append(("yfinal", j - 2))
                    dveo += [("yfinal", NC8 - 2), ("yfinal", NC8 - 1)]
                if has_a:
                    if not has_p2:
                        dveo.append(("evac", 0))
                    dveo += [("o2", 0), ("qg", 0)]
                    for hh in range(1, HG):
                        dveo += [("evac", hh), ("o2", hh), ("qg", hh)]
                    dveo += [("a4", 0), ("a4", 1)]
                    if has_p2:
                        dveo.append(("sacc", 0))
                    dveo.append(("a4", 2))
                    for c in range(NC8):
                        dveo += [("stm", c), ("kdec", c), ("qs", c)]
                        if c >= 2:
                            dveo.append(("carry", c - 2))
                        for x in range(3, NC8):
                            if prevslot[x] == c:
                                dveo.append(("a4", x))
                    dveo += [("carry", NC8 - 2), ("carry", NC8 - 1), ("cms", 0)]
                elif has_p2:
                    dveo.append(("sacc", 0))
                vtg = {op: vS + i + 1 for i, op in enumerate(dveo)}

                aco = []
                if has_p2:
                    aco += [("ysq", hh) for hh in range(HG)]
                if has_a:
                    aco += [("e1b", 0), ("dec", 0), ("e1b", 1), ("dec", 1),
                            ("e1b", 2), ("dec", 2)]
                    for c in range(NC8):
                        aco += [("yie", c), ("cst", c)]
                        if c + 3 < NC8:
                            aco += [("e1b", c + 3), ("dec", c + 3)]
                atg = {op: aS + i + 1 for i, op in enumerate(aco)}
                g_qs = [gS + c + 1 for c in range(NC8)]

                if has_p2:
                    def g_crb(gp, g=g):
                        gp.wait_ge(sC.h, g)
                        gp.dma_start(out=CRB, in_=ccout.ap()).then_inc(sW.h, 16)
                    k.gp(g_crb)
                if has_a:
                    kf_prev = sSL[10].n
                    kf_tot = sSL[10].inc(16 * 9)

                    def p1_gpq(gp, g=g, xin=xin, has_p2=has_p2):
                        gp.wait_ge(sX.h, xin)
                        if has_p2:
                            gp.wait_ge(sW.h, xCRB[g - 1])
                        gp.collective_compute(
                            "AllReduce", ALU.add,
                            replica_groups=[[0, 1], [2, 3], [4, 5], [6, 7]],
                            ins=[ccin.ap().opt()],
                            outs=[ccout.ap().opt()],
                        ).then_inc(sC.h, 1)
                    sC.inc()
                    k.gp(p1_gpq)

                def mk_slab(gg, hh, wait_qp, wait_abs=None):
                    sem = sQW[hh % 2]
                    pv = sem.n
                    tgt = sem.inc(16)
                    slabT[(gg, hh)] = tgt

                    def f(s, gg=gg, hh=hh, wait_qp=wait_qp, pv=pv, sem=sem,
                          ptg=ptg, wait_abs=wait_abs):
                        if wait_abs is not None:
                            s.wait_ge(sP.h, wait_abs)
                        if wait_qp is not None:
                            s.wait_ge(sP.h, ptg[wait_qp])
                        s.wait_ge(sem.h, pv)
                        hgl = gg * HG + hh
                        buf = WSA if hh % 2 == 0 else WSB
                        s.dma_start(out=buf[:, :],
                                    in_=dwq[hgl * 128:(hgl + 1) * 128, :]
                                    ).then_inc(sem.h, 16)
                    return f

                if has_a:
                    if g == 0:
                        k.sy(mk_slab(0, 0, None, wait_abs=trT))
                        k.sy(mk_slab(0, 1, None, wait_abs=trT))

                    vKR = vtg[("krope", NC8 - 1)] if g == 0 else None

                    def a_sy1(s, g=g, kf_prev=kf_prev, vKR=vKR):
                        if g == 0:
                            s.wait_ge(sV.h, vZ)
                        s.wait_ge(sSL[10].h, kf_prev)
                        s.dma_start(out=GR4, in_=GFT[g * HG:(g + 1) * HG, :]
                                    ).then_inc(sSL[10].h, 16)
                        if g == 0:
                            s.wait_ge(sV.h, vKR)
                        for c in range(NC8):
                            s.dma_start_transpose(
                                KFG[:, c * 128:(c + 1) * 128],
                                KTB[:, c * 1024 + g * 128:c * 1024 + (g + 1) * 128],
                            ).then_inc(sSL[10].h, 16)
                    k.sy(a_sy1)
                    k.sy(mk_slab(g, 2, ("qp", 0)))
                    k.sy(mk_slab(g, 3, ("qp", 1)))
                if has_p2:
                    pk_prev = sSL[11].n
                    yfin_all = vtg[("yfinal", NC8 - 1)]

                    def p2_sy2(s, g=g, yfin_all=yfin_all, pk_prev=pk_prev):
                        s.wait_ge(sV.h, yfin_all)
                        s.wait_ge(sSL[11].h, pk_prev)
                        s.dma_start(out=ypark[g - 1, :, :], in_=YGRp(g - 1)
                                    ).then_inc(sSL[11].h, 16)
                    xPark[g - 1] = sSL[11].inc(16)
                    k.sy(p2_sy2)
                if has_a:
                    if g + 1 < NG:
                        k.sy(mk_slab(g + 1, 0, ("qp", 2)))
                        k.sy(mk_slab(g + 1, 1, ("qp", 3)))
                    vcm = vtg[("cms", 0)]

                    def p1_sy(s, vcm=vcm, g=g):
                        s.wait_ge(sV.h, vcm)
                        s.wait_ge(sC.h, g)
                        s.dma_start(out=ccin.ap(), in_=CMS).then_inc(sX.h, 16)
                    k.sy(p1_sy)

                def s_pe(t, g=g, peo=peo, ptg=ptg, pa_tg=pa_tg, vtg=vtg, atg=atg,
                         prevT=prevT, g_qs=g_qs,
                         kf_tot=kf_tot if has_a else None):
                    for op in peo:
                        nm, c = op
                        if nm == "qp":
                            t.wait_ge(sQW[c % 2].h, slabT[(g, c)])
                            if c == 0 and g == 0:
                                t.wait_ge(sV.h, vTRtot)
                                t.wait_ge(sA.h, aTRtot)
                            if c == 0 and prevT is not None:
                                t.wait_ge(sV.h, prevT["vtg"][("qs", 4)])
                            if c == 1 and prevT is not None:
                                t.wait_ge(sA.h, prevT["atg"][("cst", 7)])
                            if c >= 2:
                                t.wait_ge(sV.h, vtg[("evac", c - 2)])
                            buf = WSA if c % 2 == 0 else WSB
                            dst = pqa if c % 2 == 0 else prt
                            for f in range(FIN):
                                for px in range(2):
                                    ins = t.matmul(
                                        dst[:, px * 512:(px + 1) * 512],
                                        buf[:, f * 128:(f + 1) * 128],
                                        R2[:, f * 1024 + px * 512:f * 1024 + (px + 1) * 512],
                                        start=(f == 0), stop=(f == FIN - 1))
                            ins.then_inc(sP.h, 1)
                        elif nm == "rot":
                            t.wait_ge(sV.h, vtg[("evac", c)])
                            if c == 0 and prevT is not None:
                                t.wait_ge(sV.h, prevT["vtg"][("qs", 6)])
                            for px in range(2):
                                ins = t.matmul(pst[:, px * 512:(px + 1) * 512],
                                               PSW[:, :],
                                               QRW[:, px * 512:(px + 1) * 512],
                                               start=True, stop=True)
                            ins.then_inc(sP.h, 1)
                        elif nm == "inter":
                            t.wait_ge(sV.h, vtg[("pvb", c)])
                            t.wait_ge(sV.h, prevT["vtg"][("qs", c)])
                            if c > 1:
                                t.wait_ge(sV.h, vtg[("yfinal", c - 2)])
                            yb = py4 if c % 2 == 0 else pc4
                            pvs = PRVB2 if c % 2 == 0 else PRVB2B
                            for hh in range(HG):
                                ins = t.matmul(yb[:, hh * 128:(hh + 1) * 128],
                                               pvs[:, hh * 128:(hh + 1) * 128],
                                               QS[:, c * 512 + hh * 128:c * 512 + (hh + 1) * 128],
                                               start=(hh == 0), stop=(hh == 3))
                            ins.then_inc(sP.h, 1)
                        elif nm == "stgb":
                            stb, gbb = SBK(c)
                            if c == 0:
                                t.wait_ge(sSL[10].h, kf_tot)
                                t.wait_ge(sV.h, vtg[("qg", HG - 1)])
                            elif c == 1:
                                t.wait_ge(sV.h, vtg[("evac", 2)])
                            elif c == 2:
                                if ("sacc", 0) in vtg:
                                    t.wait_ge(sV.h, vtg[("sacc", 0)])
                            else:
                                pc = prevslot[c]
                                t.wait_ge(sV.h, vtg[("qs", pc)])
                            t.matmul(stb, KFG[:, c * 128:(c + 1) * 128],
                                     QG[:, c * 512:(c + 1) * 512], start=True, stop=True)
                            for hh in range(HG):
                                ins = t.matmul(gbb[:, hh * 128:(hh + 1) * 128],
                                               SEL4[:, hh * 128:(hh + 1) * 128],
                                               GR4[:, c * 128:(c + 1) * 128],
                                               start=(hh == 0), stop=(hh == 3))
                            ins.then_inc(sPa.h, 1)
                        elif nm == "ymm":
                            t.wait_ge(sV.h, vtg[("stm", c)])
                            if c > 0:
                                t.wait_ge(sA.h, atg[("yie", c - 1)])
                            t.matmul(prt[:, 0:512],
                                     VTB[:, c * 1024 + g * 128:c * 1024 + (g + 1) * 128],
                                     SM4p(c), start=True, stop=True).then_inc(sP.h, 1)
                        elif nm == "cmm":
                            t.wait_ge(sV.h, vtg[("kdec", c)])
                            if c > 0:
                                t.wait_ge(sA.h, atg[("cst", c - 1)])
                            for hh in range(HG):
                                ins = t.matmul(prt[:, 512 + hh * 128:512 + (hh + 1) * 128],
                                               KD4p(c)[:, hh * 128:(hh + 1) * 128],
                                               VTB[:, c * 1024 + g * 128:c * 1024 + (g + 1) * 128],
                                               start=(hh == 0), stop=(hh == 3))
                            ins.then_inc(sP.h, 1)
                        else:
                            hh = c
                            t.wait_ge(sA.h, atg[("ysq", hh)])
                            hgl = (g - 1) * HG + hh
                            t.matmul(py4[0:32, :], EYE[:, hgl * 32:(hgl + 1) * 32],
                                     SQY2[:, 0:512],
                                     start=(hh == 0), stop=(hh == 3))
                            t.matmul(pc4[0:32, :], EYE[:, hgl * 32:(hgl + 1) * 32],
                                     SQY2[:, 512:1024],
                                     start=(hh == 0), stop=(hh == 3)).then_inc(sP.h, 1)
                if has_a:
                    sP.inc(2 * HG + 2 * NC8)
                    sPa.inc(NC8)
                if has_p2:
                    sP.inc(NC8 + HG)
                k.pe(s_pe)

                def s_dv(v, g=g, dveo=dveo, vtg=vtg, ptg=ptg, pa_tg=pa_tg,
                         atg=atg, g_qs=g_qs, gS=gS, prevT=prevT):
                    for op in dveo:
                        nm, c = op
                        if nm == "evac":
                            v.wait_ge(sP.h, ptg[("qp", c)])
                            src = pqa if c % 2 == 0 else prt
                            v.tensor_copy(QRW, src[:, :]).then_inc(sV.h, 1)
                        elif nm == "o2":
                            v.wait_ge(sP.h, ptg[("rot", c)])
                            v.tensor_tensor(QRW, QRW, COSB, ALU.mult)
                            v.tensor_tensor(O2S, pst[:, :], SINM, ALU.mult).then_inc(sV.h, 1)
                        elif nm == "qg":
                            qg3 = QG.rearrange("p (c h i) -> p c h i", c=8, h=4)[:, :, c, :]
                            v.tensor_tensor(qg3,
                                            QRW.rearrange("p (c i) -> p c i", c=8),
                                            O2S.rearrange("p (c i) -> p c i", c=8),
                                            ALU.add).then_inc(sV.h, 1)
                        elif nm == "krope":
                            kc = KTB[:, c * 1024:(c + 1) * 1024]
                            kc3 = kc.rearrange("p (kv d) -> p kv d", kv=8)
                            cos3 = CTT[:, c * 128:(c + 1) * 128].unsqueeze(1)\
                                .broadcast_to([128, 8, 128])
                            sin_lo = STT[:, c * 128:c * 128 + 64].unsqueeze(1)\
                                .broadcast_to([128, 8, 64])
                            sin_hi = STT[:, c * 128 + 64:(c + 1) * 128].unsqueeze(1)\
                                .broadcast_to([128, 8, 64])
                            kr13 = YGRB[:, 0:1024].rearrange("p (kv d) -> p kv d", kv=8)
                            kr23 = YGRB[:, 1024:2048].rearrange("p (kv d) -> p kv d", kv=8)
                            v.tensor_tensor(kr23[:, :, 0:64], kc3[:, :, 64:128],
                                            sin_lo, ALU.mult)
                            v.tensor_tensor(kr23[:, :, 64:128], kc3[:, :, 0:64],
                                            sin_hi, ALU.mult)
                            v.tensor_tensor(kr13, kc3, cos3, ALU.mult)
                            v.tensor_add(kc, YGRB[:, 0:1024],
                                         YGRB[:, 1024:2048]).then_inc(sV.h, 1)
                        elif nm == "prvi":
                            v.wait_ge(sW.h, xCRB[g - 1])
                            v.tensor_scalar_mul(PRV[:, :], CRB,
                                                RECVM[:, :]).then_inc(sV.h, 1)
                        elif nm == "pvb":
                            if c >= 2:
                                v.wait_ge(sP.h, ptg[("inter", c - 2)])
                            elif prevT is not None and prevT["had_p2"]:
                                v.wait_ge(sP.h, prevT["ptg"][("inter", NC8 - 2 + c)])
                            dstp = PRVB2 if c % 2 == 0 else PRVB2B
                            v.tensor_copy(dstp, PRVp(c)[:, :]).then_inc(sV.h, 1)
                        elif nm == "scan":
                            for hh in range(HG):
                                hgl = (g - 1) * HG + hh
                                ins = v.scalar_tensor_tensor(
                                    PRVp(c + 1)[:, hh * 128:(hh + 1) * 128],
                                    PRVp(c)[:, hh * 128:(hh + 1) * 128],
                                    CDB[:, c * 32 + hgl:c * 32 + hgl + 1],
                                    CST[:, c * 512 + hh * 128:c * 512 + (hh + 1) * 128],
                                    ALU.mult, ALU.add)
                            ins.then_inc(sV.h, 1)
                        elif nm == "yfinal":
                            v.wait_ge(sP.h, ptg[("inter", c)])
                            yb = py4 if c % 2 == 0 else pc4
                            yg3 = YGRp(g - 1).rearrange(
                                "p (h c i) -> p h c i", h=4, c=8)[:, :, c, :]
                            v.tensor_tensor(yg3,
                                            yb[:, :].rearrange("p (h i) -> p h i", h=4),
                                            yg3, ALU.add).then_inc(sV.h, 1)
                        elif nm == "a4":
                            gbb = SBK(c)[1]
                            v.wait_ge(sPa.h, pa_tg[("stgb", c)])
                            v.tensor_tensor(
                                A4Sp(c).rearrange("p (h i) -> p h i", h=4),
                                gbb.rearrange("p (h i) -> p h i", h=4),
                                MSKN.unsqueeze(1).broadcast_to([128, 4, 128]),
                                ALU.add).then_inc(sV.h, 1)
                        elif nm == "stm":
                            stb = SBK(c)[0]
                            v.wait_ge(sA.h, atg[("dec", c)])
                            if c > 1:
                                v.wait_ge(sP.h, ptg[("ymm", c - 2)])
                            v.tensor_tensor(SM4p(c), stb, A4Sp(c),
                                            ALU.mult).then_inc(sV.h, 1)
                        elif nm == "kdec":
                            if c > 1:
                                v.wait_ge(sP.h, ptg[("cmm", c - 2)])
                            kin = KTB[:, c * 1024 + g * 128:c * 1024 + (g + 1) * 128]
                            v.tensor_tensor(
                                KD4p(c).rearrange("p (h d) -> p h d", h=4),
                                kin.unsqueeze(1).broadcast_to([128, 4, 128]),
                                EDT[:, c * 32 + g * 4:c * 32 + g * 4 + 4]
                                    .unsqueeze(2).broadcast_to([128, 4, 128]),
                                ALU.mult).then_inc(sV.h, 1)
                        elif nm == "carry":
                            v.wait_ge(sA.h, atg[("cst", c)])
                            if c == 0:
                                v.memset(CARRY[:, :], 0.0)
                            for hh in range(HG):
                                hgl = g * HG + hh
                                ins = v.scalar_tensor_tensor(
                                    CARRY[:, hh * 128:(hh + 1) * 128],
                                    CARRY[:, hh * 128:(hh + 1) * 128],
                                    CDB[:, c * 32 + hgl:c * 32 + hgl + 1],
                                    CST[:, c * 512 + hh * 128:c * 512 + (hh + 1) * 128],
                                    ALU.mult, ALU.add)
                            ins.then_inc(sV.h, 1)
                        elif nm == "qs":
                            v.wait_ge(sA.h, atg[("e1b", c)])
                            if ("inter", c) in ptg:
                                v.wait_ge(sP.h, ptg[("inter", c)])
                            v.tensor_tensor(QS[:, c * 512:(c + 1) * 512],
                                            QG[:, c * 512:(c + 1) * 512], SBK(c)[1],
                                            ALU.mult).then_inc(sV.h, 1)
                        elif nm == "cms":
                            v.tensor_scalar_mul(CMS, CARRY[:, :],
                                                SENDM[:, :]).then_inc(sV.h, 1)
                        else:
                            v.wait_ge(sP.h, ptg[("ssq", HG - 1)])
                            v.tensor_tensor(SSQ[:, 0:512], SSQ[:, 0:512],
                                            py4[0:32, :], ALU.add)
                            v.tensor_tensor(SSQ[:, 512:1024], SSQ[:, 512:1024],
                                            pc4[0:32, :], ALU.add).then_inc(sV.h, 1)
                sV.inc(len(dveo))
                k.dv(s_dv)

                def s_ac(sc, g=g, aco=aco, vtg=vtg, ptg=ptg, g_qs=g_qs,
                         has_p2=has_p2):
                    for nm, c in aco:
                        if nm == "ysq":
                            if c == 0:
                                sc.wait_ge(sV.h, vtg[("yfinal", NC8 - 1)])
                            else:
                                sc.wait_ge(sP.h, ptg[("ssq", c - 1)])
                            ys = YGRp(g - 1)
                            sc.square(SQY2, ys[:, c * 1024:(c + 1) * 1024]
                                      ).then_inc(sA.h, 1)
                        elif nm == "dec":
                            for hh in range(HG):
                                hgl = g * HG + hh
                                ins = sc.activation(
                                    A4Sp(c)[:, hh * 128:(hh + 1) * 128],
                                    A4Sp(c)[:, hh * 128:(hh + 1) * 128], AF.Exp,
                                    bias=DTTc[:, c * 32 + hgl:c * 32 + hgl + 1])
                            ins.then_inc(sA.h, 1)
                        elif nm == "e1b":
                            gbb = SBK(c)[1]
                            sc.wait_ge(sV.h, vtg[("a4", c)])
                            sc.activation(gbb, gbb, AF.Exp).then_inc(sA.h, 1)
                        elif nm == "yie":
                            sc.wait_ge(sP.h, ptg[("ymm", c)])
                            if c == 0 and g >= 2:
                                sc.wait_ge(sSL[11].h, xPark[g - 2])
                            yg3 = YGRp(g).rearrange(
                                "p (h c i) -> p h c i", h=4, c=8)[:, :, c, :]
                            sc.copy(yg3,
                                    prt[:, 0:512].rearrange("p (h i) -> p h i", h=4)
                                    ).then_inc(sA.h, 1)
                        else:
                            sc.wait_ge(sP.h, ptg[("cmm", c)])
                            if has_p2 and c < NC8 - 1:
                                sc.wait_ge(sV.h, vtg[("scan", c)])
                            sc.copy(CST[:, c * 512:(c + 1) * 512],
                                    prt[:, 512:1024]).then_inc(sA.h, 1)
                sA.inc(len(aco))
                k.ac(s_ac)

                prevT = dict(vtg=vtg, atg=atg, ptg=ptg, gqs=g_qs, had_p2=has_p2)
                if not has_a:
                    break
            k.emit()


        dbg_tap("ssq", SCR[0:32, 0:1024])

        with (
            nc.psum_tensor("pga", [128, 1024], F32) as pga,
            nc.psum_tensor("pgb", [128, 1024], F32) as pgb,
            nc.psum_tensor("prb", [128, 1024], F32) as prb,
        ):
            k = Blk(nc)
            vr = sV.n
            pk_all = sSL[11].n
            def r_dv(v, pk_all=pk_all):
                v.wait_ge(sSL[11].h, pk_all)
                v.tensor_scalar(RSTD, SSQ, 1.0 / D, RMS_EPS, ALU.mult, ALU.add)
                v.drain().then_inc(sV.h, 1)
            sV.inc()
            k.dv(r_dv)
            vrT = sV.n
            def r_ac(sc):
                sc.wait_ge(sV.h, vrT)
                sc.activation(RSTD, RSTD, AF.Ln)
                sc.activation(RSTD, RSTD, AF.Exp, scale=-0.5).then_inc(sA.h, 1)
            sA.inc()
            k.ac(r_ac)
            k.emit(bar=False)

            aR = sA.n
            k = Blk(nc)
            for g in range(NG):
                w0 = sW.n
                x0 = sX.n
                p0 = sP.n
                v0 = sV.n
                a0 = sA.n
                slabt = [sSL[hh % 2].inc(16) for hh in range(HG)]
                yht = [sSL[8 + hh % 2].inc(16) for hh in range(HG)]
                gr4t = sSL[10].inc(16)
                def g_gp(gp, g=g, p0=p0, slabt=slabt):
                    for hh in range(HG):
                        hgl = g * HG + hh
                        buf = WSA if hh % 2 == 0 else WSB
                        if hh >= 2:
                            gp.wait_ge(sP.h, p0 + (hh - 2) * 2 + 1)
                        elif g > 0:
                            gp.wait_ge(sP.h, p0 - 3 if hh == 0 else p0 - 1)
                        gp.dma_start(out=buf[:, :],
                                     in_=dwg[hgl * 128:(hgl + 1) * 128, :]
                                     ).then_inc(sSL[hh % 2].h, 16)
                k.gp(g_gp)

                yh_prev = (sSL[8].n - (32 if g > 0 else 0), sSL[9].n - (32 if g > 0 else 0))
                def g_sy(s, g=g, v0=v0, gr4t=gr4t, yht=yht, p0=p0, aR=aR):
                    if g == 0:
                        s.wait_ge(sA.h, aR)
                        s.wait_ge(sV.h, v0)
                    else:
                        s.wait_ge(sP.h, p0)
                    s.wait_ge(sSL[10].h, gr4t - 16)
                    s.dma_start(out=GR4, in_=RSTD[g * HG:(g + 1) * HG, :]
                                ).then_inc(sSL[10].h, 16)
                    for hh in range(HG):
                        if hh >= 2:
                            s.wait_ge(sV.h, v0 + (hh - 2) * 3 + 3)
                        else:
                            s.wait_ge(sSL[8 + hh].h, yht[hh] - 16)
                            if g > 0:
                                s.wait_ge(sV.h, v0 - 5 + 3 * hh)
                        s.dma_start(out=YHp[hh % 2],
                                    in_=ypark[g, :, hh * 1024:(hh + 1) * 1024]
                                    ).then_inc(sSL[8 + hh % 2].h, 16)
                k.sy(g_sy)

                def g_pe(t, g=g, v0=v0, a0=a0, slabt=slabt, gr4t=gr4t):
                    for hh in range(HG):
                        t.wait_ge(sSL[hh % 2].h, slabt[hh])
                        ps = pga if hh % 2 == 0 else pgb
                        buf = WSA if hh % 2 == 0 else WSB
                        if hh == 0:
                            t.wait_ge(sA.h, a0 - 1)
                        elif hh == 1:
                            t.wait_ge(sA.h, a0)
                        if hh >= 2:
                            t.wait_ge(sV.h, v0 + (hh - 2) * 3 + 3)
                        for f in range(FIN):
                            for p2 in range(2):
                                ins = t.matmul(ps[:, p2 * 512:(p2 + 1) * 512],
                                         buf[:, f * 128:(f + 1) * 128],
                                         R2[:, f * 1024 + p2 * 512:f * 1024 + (p2 + 1) * 512],
                                         start=(f == 0), stop=(f == FIN - 1))
                        ins.then_inc(sP.h, 1)
                        t.wait_ge(sSL[10].h, gr4t)
                        if hh > 0:
                            t.wait_ge(sV.h, v0 + (hh - 1) * 3 + 3)
                        else:
                            t.wait_ge(sV.h, v0)
                        for p2 in range(2):
                            ins = t.matmul(prb[:, p2 * 512:(p2 + 1) * 512],
                                     SEL4[:, hh * 128:hh * 128 + 128],
                                     GR4[:, p2 * 512:(p2 + 1) * 512],
                                     start=True, stop=True)
                        ins.then_inc(sP.h, 1)
                sP.inc(2 * HG)
                k.pe(g_pe)

                def g_ac(sc, g=g, p0=p0, a0=a0, v0=v0):
                    for hh in range(HG):
                        sc.wait_ge(sP.h, p0 + 2 * hh + 1)
                        if hh > 0:
                            sc.wait_ge(sV.h, v0 + (hh - 1) * 3 + 1)
                        elif g > 0:
                            sc.wait_ge(sV.h, v0 - 2)
                        ps = pga if hh % 2 == 0 else pgb
                        sc.activation(SILt, ps[:, :], AF.Silu).then_inc(sA.h, 1)
                sA.inc(HG)
                k.ac(g_ac)

                def g_dv(v, g=g, p0=p0, v0=v0, a0=a0, yht=yht):
                    for hh in range(HG):
                        hgl = g * HG + hh
                        ps = pga if hh % 2 == 0 else pgb
                        v.wait_ge(sA.h, a0 + hh + 1)
                        v.wait_ge(sSL[8 + hh % 2].h, yht[hh])
                        v.tensor_tensor(OYt, YHp[hh % 2], SILt, ALU.mult).then_inc(sV.h, 1)
                        v.wait_ge(sP.h, p0 + 2 * hh + 2)
                        v.scalar_tensor_tensor(R1[:, hgl * 1024:(hgl + 1) * 1024],
                                               OYt, GNWC[:, :], prb[:, :],
                                               ALU.mult, ALU.mult)
                        v.drain().then_inc(sV.h, 1)
                        v.drain().then_inc(sV.h, 1)
                sV.inc(3 * HG)
                k.dv(g_dv)
            k.emit(bar=False)

        dbg_tap("yg", R1[:, 0:2048])

        with (
            nc.psum_tensor("poa", [128, 1024], F32) as poa,
            nc.psum_tensor("pob", [128, 1024], F32) as pob,
        ):
            k = Blk(nc)
            p0 = sP.n
            v0 = sV.n
            slabt6 = [sSL[ob % 2].inc(16) for ob in range(FIN)]
            xpt6 = [sSL[8 + ob % 2].inc(16) for ob in range(FIN)]

            def o_gp(gp, p0=p0, slabt6=slabt6):
                for ob in range(FIN):
                    buf = WSA if ob % 2 == 0 else WSB
                    if ob >= 2:
                        gp.wait_ge(sP.h, p0 + ob - 1)
                    else:
                        gp.wait_ge(sP.h, p0)
                    gp.dma_start(out=buf[:, :],
                                 in_=dwo[ob * 128:(ob + 1) * 128, :]
                                 ).then_inc(sSL[ob % 2].h, 16)
            k.gp(o_gp)

            def o_sy(s, v0=v0, xpt6=xpt6):
                for ob in range(FIN):
                    if ob >= 2:
                        s.wait_ge(sV.h, v0 + ob - 1)
                    else:
                        s.wait_ge(sV.h, v0)
                    s.dma_start(out=XPTp[ob % 2],
                                in_=xpark[ob * 128:(ob + 1) * 128, :]
                                ).then_inc(sSL[8 + ob % 2].h, 16)
            k.sy(o_sy)

            def o_pe(t, v0=v0, slabt6=slabt6):
                t.wait_ge(sV.h, v0)
                for ob in range(FIN):
                    t.wait_ge(sSL[ob % 2].h, slabt6[ob])
                    if ob >= 2:
                        t.wait_ge(sV.h, v0 + ob - 1)
                    ps = poa if ob % 2 == 0 else pob
                    buf = WSA if ob % 2 == 0 else WSB
                    for hd in range(FIN):
                        for p2 in range(2):
                            ins = t.matmul(ps[:, p2 * 512:(p2 + 1) * 512],
                                     buf[:, hd * 128:(hd + 1) * 128],
                                     R1[:, hd * 1024 + p2 * 512:hd * 1024 + (p2 + 1) * 512],
                                     start=(hd == 0), stop=(hd == FIN - 1))
                    ins.then_inc(sP.h, 1)
            sP.inc(FIN)
            k.pe(o_pe)

            def o_dv(v, p0=p0, xpt6=xpt6):
                for ob in range(FIN):
                    v.wait_ge(sP.h, p0 + ob + 1)
                    v.wait_ge(sSL[8 + ob % 2].h, xpt6[ob])
                    ps = poa if ob % 2 == 0 else pob
                    v.tensor_tensor(R2[:, ob * 1024:(ob + 1) * 1024], ps[:, :],
                                    XPTp[ob % 2], ALU.add).then_inc(sV.h, 1)
            oDvB = v0
            sV.inc(FIN)
            k.dv(o_dv)
            k.emit(bar=False)

        dbg_tap("xnew", R2[:, 0:2048])

        emit_ln(R2, None, "l3", fold=True, srcw=oDvB)
        emit_ffn(R2, R2, "f2", final=True, lncorr=True)

        k = Blk(nc)
        xF = sX.n
        s8, s9 = sSL[8].n, sSL[9].n
        k.sy(lambda s: (s.wait_ge(sX.h, xF), s.wait_ge(sSL[8].h, s8),
                        s.wait_ge(sSL[9].h, s9)))
        k.gp(lambda g: g.wait_ge(sX.h, xF))
        k.emit()
        Blk.flush(nc)

    return nc, dbg_names


def _rope_tables(positions):
    inv = 1.0 / 10000.0 ** (np.arange(0, D, 2, dtype=np.float32) / D)
    ang = positions[:, None].astype(np.float32) * inv
    cos = np.concatenate([np.cos(ang), np.cos(ang)], axis=-1)
    sin = np.concatenate([np.sin(ang), np.sin(ang)], axis=-1)
    return cos, sin


def kernel(L0_attn_hidden_states, L0_attn_Wq, L0_attn_Wk, L0_attn_Wv, L0_attn_Wo,
           L0_attn_Wg, L0_attn_Win, L0_attn_dt_bias, L0_attn_A_log_bias,
           L0_attn_gnorm_w, L0_ffn_route_w, L0_ffn_A, L0_ffn_Bw, _debug_taps=()):
    f32 = np.float32
    X = np.ascontiguousarray(L0_attn_hidden_states.reshape(8 * T, HID))

    def blockT(W):
        return np.ascontiguousarray(
            W.reshape(32, 128, 32, 128).transpose(0, 3, 2, 1).reshape(32 * 128, 4096))

    wq = blockT(L0_attn_Wq)
    wg = blockT(L0_attn_Wg)
    wo = blockT(L0_attn_Wo)
    wvk0 = np.concatenate([L0_attn_Wv.T, L0_attn_Wk.T], axis=1)
    wvk = np.ascontiguousarray(
        wvk0.reshape(32, 128, 4, 512).transpose(2, 0, 1, 3).reshape(4 * 32 * 128, 512))
    win = np.ascontiguousarray(
        L0_attn_Win.reshape(32, 32, 128).transpose(2, 1, 0).reshape(128, 1024))
    aall = np.ascontiguousarray(
        L0_ffn_A.reshape(128, 32, 128).transpose(2, 1, 0).reshape(128, 4096))
    route = np.ascontiguousarray(
        L0_ffn_route_w.reshape(8, 32, 128).transpose(2, 1, 0).reshape(128, 256))
    bw = np.ascontiguousarray(L0_ffn_Bw.transpose(0, 2, 1).reshape(128, HID))

    psw = np.zeros((128, 128), f32)
    for m in range(128):
        psw[(m + 64) % 128, m] = 1.0
    idf = np.eye(128, dtype=f32)
    s127 = np.zeros((128, 128), f32); s127[127, :] = 1.0
    sel4 = np.zeros((4, 512), f32)
    for hh in range(4):
        sel4[hh, hh * 128:(hh + 1) * 128] = 1.0
    eye = np.zeros((128, 1024), f32)
    for h in range(32):
        eye[:, h * 32 + h] = 1.0
    mskn = np.where(np.arange(128)[:, None] > np.arange(128)[None, :], -1e30, 0.0
                    ).astype(f32)
    bf = lambda a: np.ascontiguousarray(a.astype(bfloat16))

    in_maps = []
    for c in range(8):
        xc = np.ascontiguousarray(X[c * T:(c + 1) * T, :].T)
        pos = (c % 2) * T + np.arange(T)
        cos, sin = _rope_tables(pos)
        cosf = bf(cos.T)
        sinm_f = sin.T.copy(); sinm_f[0:64, :] *= -1.0
        sinf = bf(sinm_f)
        cost = np.zeros((128, NC8 * D), f32)
        sint = np.zeros((128, NC8 * D), f32)
        for cc in range(NC8):
            cost[:, cc * D:(cc + 1) * D] = cos[cc * 128:(cc + 1) * 128, :]
            sm = sin[cc * 128:(cc + 1) * 128, :].copy()
            sm[:, 0:64] *= -1.0
            sint[:, cc * D:(cc + 1) * D] = sm
        parity = 1.0 if c % 2 == 0 else 0.0
        in_maps.append({
            "dx": bf(xc), "dwq": bf(wq), "dwg": bf(wg), "dwo": bf(wo),
            "dwvk": bf(wvk), "dwin": bf(win),
            "daall": bf(aall), "droute": bf(route), "dbw": bf(bw),
            "dcosf": cosf, "dsinf": sinf, "dcost": bf(cost), "dsint": bf(sint),
            "dpsw": bf(psw), "didf": idf, "didb": bf(idf), "ds127": s127,
            "dsel4": sel4, "deye": bf(eye), "dmskn": bf(mskn),
            "ddtb": np.ascontiguousarray(L0_attn_dt_bias.reshape(32, 1)),
            "dalog": np.ascontiguousarray(L0_attn_A_log_bias.reshape(32, 1)),
            "dgnw": np.ascontiguousarray(L0_attn_gnorm_w.reshape(128, 1)),
            "dsendm": np.full((128, 1), parity, f32),
            "drecvm": np.full((128, 1), 1.0 - parity, f32),
        })

    nc, dbg_names = build(debug_taps=_debug_taps)
    res = run_bass_kernel_spmd(nc, in_maps, core_ids=list(range(8)))

    out = np.empty((8 * T, HID), f32)
    for c in range(8):
        out[c * T:(c + 1) * T, :] = res.results[c]["dout"].T
    if _debug_taps:
        kernel.debug = [{n: r.get("dbg_" + n) for n in dbg_names}
                        for r in res.results]
    return out.reshape(4, 2048, HID)
